# version 08
# baseline (speedup 1.0000x reference)
import numpy as np
import concourse.bass as bass
import concourse.mybir as mybir
from concourse.bass import AP
from concourse.bass_utils import run_bass_kernel_spmd

F32 = mybir.dt.float32
BF16 = mybir.dt.bfloat16
AF = mybir.ActivationFunctionType
ALU = mybir.AluOpType
AXX = mybir.AxisListType.X

D = 1024
T = 4096
H = 16
G = 4
HID = 4096
TOWN = 2048
KP = 96
EPS = 1e-5
CLAMP = float(2.0 ** 120)
SWAP32 = [i ^ 1 for i in range(32)]

OWN_CHUNKS = {0: [0, 3, 4, 7], 1: [1, 2, 5, 6]}
OWN_RANGES = {h: [(c * 512, (c + 1) * 512) for c in OWN_CHUNKS[h]] for h in (0, 1)}
QCHUNKS = {h: [c * 512 for c in OWN_CHUNKS[h]] for h in (0, 1)}
NBRUN = [2, 4, 6, 8]

_STAGE = 6
_DEBUG = False
_ATT_CH = 4
_ATT_H = 16
_ATT_FORCE_HALF = None


SBUF_BASE = 17024


def _mmap():
    m = {}
    cur = [SBUF_BASE]

    def add(name, nbytes, at=None):
        if at is None:
            at = cur[0]
            cur[0] += (nbytes + 31) & ~31
        m[name] = at
        return at

    add("xnT_own", 8 * TOWN * 2)
    add("vp", 32 * 272 * 2)
    add("qt", 16 * TOWN * 2)
    add("kt", 4 * T * 2)
    add("maskt", 4 * 512 * 2)
    add("bq2s", 8 * 4); add("bk2s", 2 * 4); add("b12s", 32 * 4)
    add("bvbs", 256 * 2); add("bobs", D * 2)
    add("b2rs", D * 4)
    add("alpha1s", 32 * 4)
    add("ones_l", 128 * 2)
    add("cb", 16 * 4)
    add("cbt", 64 * 4)
    add("cqs", TOWN * 2); add("sqs", TOWN * 2)
    ra = cur[0]
    m["_RA"] = ra
    add("xbuf0", 0, ra); add("xbuf1", 0, ra + 16384)
    add("xtok0", 0, m["qt"] + 16384); add("xtok1", 0, m["qt"] + 20480)
    add("lnst", 0, ra + 32768)
    add("xnT_q", 0, ra)
    add("kstq", 0, ra + 24576)
    add("shbuf", 0, ra + 28672)
    add("wqs", 0, ra)
    add("qstc", 0, ra + 16384)
    add("q2st", 0, ra + 24576)
    add("shq", 0, ra + 28672)
    add("pt0", 0, ra); add("pt1", 0, ra + 4096)
    add("astg", 0, ra + 8192)
    add("srow", 0, ra + 9216)
    add("rrow", 0, ra + 11264)
    add("rhl", 0, ra + 13312)
    add("mskb", 0, ra + 16384)
    q0 = m["qt"]
    add("cks", 0, q0); add("sks", 0, q0 + 8192)
    add("hilo0", 0, q0 + 16384); add("hilo1", 0, q0 + 32768)
    add("wkv", 0, q0 + 49152)
    add("s1t", 0, q0)
    add("s2t", 0, q0 + 32768)
    k0 = m["kt"]
    add("wos", 0, k0)
    add("hbuf", 0, k0 + 16384)
    add("xre0", 0, k0 + 24576); add("xre1", 0, k0 + 28672)
    v0 = m["vp"]
    add("oac0", 0, v0); add("oac1", 0, v0 + 4096)
    add("ost0", 0, v0 + 8192); add("ost1", 0, v0 + 12288)
    add("att0", 0, ra)
    add("att1", 0, ra + 2048)
    add("w1n0", 0, ra + 4096); add("w1n1", 0, ra + 6144)
    add("abuf", 0, ra + 8192)
    add("b2pb", 0, ra + 10240)
    add("a2hl", 0, ra + 12288)
    add("a2row", 0, ra + 16384)
    add("a2bc", 0, m["maskt"])
    m["_END"] = ra + 31744
    assert m["_END"] <= 229344
    assert m["_END"] <= 229344
    return m


class Sem:
    def __init__(self, nc, name):
        self.h = nc.alloc_semaphore(name=name)
        self.n = 0

    def inc(self, instr, k=1):
        self.n += k
        return instr.then_inc(self.h, k)


def build_program(stage=5, debug=False):
    nc = bass.Bass("TRN2", target_bir_lowering=False, detect_race_conditions=False)
    M = _mmap()

    def din(name, shape, dtype=F32):
        return nc.dram_tensor(name, list(shape), dtype, kind="ExternalInput")

    x_all = din("x_all", [T, D])
    x_own = din("x_own", [TOWN, D])
    wq = din("wq", [D, D]); wk = din("wk", [D, 256]); wv = din("wv", [D, 256])
    wo = din("wo", [D, D])
    w1t = din("w1t", [D, HID]); w2t = din("w2t", [HID, D]); w1n = din("w1n", [HID, D])
    bq2 = din("bq2", [128, 8]); bk2 = din("bk2", [128, 2]); b12 = din("b12", [128, 32])
    bvr = din("bvr", [1, 256]); bor = din("bor", [1, D]); b2r = din("b2r", [1, D])
    ckh = din("ckh", [128, T]); ckl = din("ckl", [128, T])
    skh = din("skh", [128, T]); skl = din("skl", [128, T])
    cqh = din("cqh", [128, TOWN]); cql = din("cql", [128, TOWN])
    sqh = din("sqh", [128, TOWN]); sql = din("sql", [128, TOWN])
    mask4 = din("mask4", [128, 4 * 2048])

    y = nc.dram_tensor("y", [TOWN, D], F32, kind="ExternalOutput")
    xn_hbm = nc.dram_tensor("xn_hbm", [T, D], BF16)
    xnown_hbm = nc.dram_tensor("xnown_hbm", [TOWN, D], BF16)
    attn_hbm = nc.dram_tensor("attn_hbm", [D, TOWN], BF16)
    oacc_hbm = nc.dram_tensor("oacc_hbm", [TOWN, D], F32)

    dbg = {}
    if debug:
        dbg["d1"] = nc.dram_tensor("dbg1", [128, 16 * TOWN], BF16, kind="ExternalOutput")
        dbg["d2"] = nc.dram_tensor("dbg2", [128, 64], F32, kind="ExternalOutput")

    def sbt(name, shape, dtype):
        return nc.alloc_sbuf_tensor_at(name, list(shape), dtype, offset=M[name])

    xnT_own = sbt("xnT_own", [128, 8 * TOWN], BF16)
    vp = sbt("vp", [128, 32 * 272], BF16)
    qt = [nc.alloc_sbuf_tensor_at(f"qt{h}", [KP, TOWN], BF16, offset=M["qt"] + h * TOWN * 2) for h in range(H)]
    kt = [nc.alloc_sbuf_tensor_at(f"kt{g}", [KP, T], BF16, offset=M["kt"] + g * T * 2) for g in range(G)]
    mskb = sbt("mskb", [128, 4 * 2048], BF16)
    bq2s = sbt("bq2s", [128, 8], F32); bk2s = sbt("bk2s", [128, 2], F32)
    b12s = sbt("b12s", [128, 32], F32)
    bvbs = sbt("bvbs", [1, 256], BF16); bobs = sbt("bobs", [1, D], BF16)
    b2rs = sbt("b2rs", [1, D], F32)
    alpha1s = sbt("alpha1s", [128, 32], F32)
    ones_l = sbt("ones_l", [128, 128], BF16)
    cbS = sbt("cb", [128, 16], F32)
    cbt = sbt("cbt", [1, 64], F32)
    cqs = sbt("cqs", [128, TOWN], BF16); sqs = sbt("sqs", [128, TOWN], BF16)
    xbuf = [sbt("xbuf0", [128, 4096], F32), sbt("xbuf1", [128, 4096], F32)]
    xtok = [sbt("xtok0", [128, 2048], BF16), sbt("xtok1", [128, 2048], BF16)]
    lnst = sbt("lnst", [128, 128], F32)
    xnT_q = sbt("xnT_q", [128, 8 * 1024], BF16)
    wkvS = sbt("wkv", [128, 4 * 1024], BF16)
    kstq = sbt("kstq", [128, 2 * 1024], BF16)
    shbuf = sbt("shbuf", [128, 1024], BF16)
    wqs = sbt("wqs", [128, 8 * 1024], BF16)
    qstc = sbt("qstc", [128, 8 * 512], BF16)
    q2st = sbt("q2st", [KP, TOWN], BF16)
    shq = sbt("shq", [128, 512], BF16)
    pt = [sbt("pt0", [128, 2048], BF16), sbt("pt1", [128, 2048], BF16)]
    astg = sbt("astg", [128, 512], BF16)
    srow = sbt("srow", [128, 512], F32)
    rrow = sbt("rrow", [128, 512], F32)
    rhl = sbt("rhl", [128, 1024], BF16)
    cks = sbt("cks", [128, T], BF16); sks = sbt("sks", [128, T], BF16)
    hilo = [sbt("hilo0", [128, T], F32), sbt("hilo1", [128, T], F32)]
    s1t = sbt("s1t", [128, 8 * 2048], BF16)
    s2t = sbt("s2t", [128, 16 * 1024], BF16)
    wos = sbt("wos", [128, 8 * 1024], BF16)
    hbuf = sbt("hbuf", [128, 16 * 256], BF16)
    xre = [sbt("xre0", [128, 1024], F32), sbt("xre1", [128, 1024], F32)]
    oac = [sbt("oac0", [128, 1024], F32), sbt("oac1", [128, 1024], F32)]
    ost = [sbt("ost0", [128, 1024], F32), sbt("ost1", [128, 1024], F32)]
    att = [sbt("att0", [128, 8 * 128], BF16), sbt("att1", [128, 8 * 128], BF16),
           nc.alloc_sbuf_tensor_at("att2", [128, 8 * 128], BF16, offset=M["w1n0"]),
           nc.alloc_sbuf_tensor_at("att3", [128, 8 * 128], BF16, offset=M["w1n1"])]
    w1nb = [sbt("w1n0", [128, 1024], BF16), sbt("w1n1", [128, 1024], BF16)]
    abuf = sbt("abuf", [128, 1024], BF16)
    b2pb = sbt("b2pb", [1, D], BF16)
    a2hl = sbt("a2hl", [1, 2048], BF16)
    a2row = sbt("a2row", [1, D], F32)
    a2bc = sbt("a2bc", [128, D], F32)

    SP, PE, ACT, DVE, PL = nc.sync, nc.tensor, nc.scalar, nc.vector, nc.gpsimd

    sLD = Sem(nc, "sLD"); sWT = Sem(nc, "sWT"); sXN = Sem(nc, "sXN")
    sTR = Sem(nc, "sTR"); sDV = Sem(nc, "sDV"); sAC = Sem(nc, "sAC")
    sPE = Sem(nc, "sPE"); sSH = Sem(nc, "sSH"); sST = Sem(nc, "sST")
    sC1 = Sem(nc, "sC1"); sC2 = Sem(nc, "sC2"); sC3 = Sem(nc, "sC3")

    x_all_t = x_all.rearrange("(n p) m -> n p m", p=128)
    x_own_t = x_own.rearrange("(n p) m -> n p m", p=128)
    xn_hbm_t = xn_hbm.rearrange("(n p) m -> n p m", p=128)
    xnown_hbm_t = xnown_hbm.rearrange("(n p) m -> n p m", p=128)
    oacc_t = oacc_hbm.rearrange("(n p) m -> n p m", p=128)
    y_t = y.rearrange("(n p) m -> n p m", p=128)

    for dst, src in ((bq2s, bq2), (bk2s, bk2), (b12s, b12), (b2rs, b2r)):
        sLD.inc(SP.dma_start(dst[:, :], src[:, :]), 16)
    sWT.inc(PL.dma_start(bvbs[:, :], bvr[:, :]), 16)
    sWT.inc(PL.dma_start(bobs[:, :], bor[:, :]), 16)

    for ti, (hsrc, lsrc, dstb) in enumerate(((ckh, ckl, cks), (skh, skl, sks),
                                             (cqh, cql, cqs), (sqh, sql, sqs))):
        n = hsrc.shape[1]
        if ti > 0:
            SP.wait_ge(sDV.h, sDV.n)
        sLD.inc(SP.dma_start(hilo[0][:, 0:n], hsrc[:, :]), 16)
        sLD.inc(SP.dma_start(hilo[1][:, 0:n], lsrc[:, :]), 16)
        DVE.wait_ge(sLD.h, sLD.n)
        DVE.tensor_tensor(hilo[0][:, 0:n], hilo[0][:, 0:n], hilo[1][:, 0:n], ALU.add)
        sDV.inc(DVE.tensor_copy(dstb[:, 0:n], hilo[0][:, 0:n]))
    DVE.memset(ones_l[:, :], 1.0)
    sDV.inc(DVE.memset(vp[:, :], 1.0))
    consts_done = sDV.n

    ln_groups = [("all", gi) for gi in range(8)] + [("own", gi) for gi in range(4)]
    grp_dve_done = []
    grp_st_done = []
    N2 = lambda: (DVE.engine_nop(), DVE.engine_nop())
    for idx, (kind, gi) in enumerate(ln_groups):
        bi = idx % 2
        src_t = x_all_t if kind == "all" else x_own_t
        dst_t = xn_hbm_t if kind == "all" else xnown_hbm_t
        if idx >= 2:
            SP.wait_ge(sDV.h, grp_dve_done[idx - 2])
        for tl in range(4):
            sLD.inc(SP.dma_start(xbuf[bi][:, tl * 1024:(tl + 1) * 1024], src_t[4 * gi + tl]), 16)
        DVE.wait_ge(sLD.h, sLD.n)
        for tl in range(4):
            DVE.bn_stats(lnst[:, 12 * tl:12 * tl + 6], xbuf[bi][:, tl * 1024:tl * 1024 + 512])
            DVE.bn_stats(lnst[:, 12 * tl + 6:12 * tl + 12], xbuf[bi][:, tl * 1024 + 512:tl * 1024 + 1024])
            DVE.bn_aggr(lnst[:, 48 + 4 * tl:50 + 4 * tl], lnst[:, 12 * tl:12 * tl + 6])
            DVE.bn_aggr(lnst[:, 50 + 4 * tl:52 + 4 * tl], lnst[:, 12 * tl + 6:12 * tl + 12])
        m1 = AP(lnst, 48, [[128, 128], [4, 4]]); v1 = AP(lnst, 49, [[128, 128], [4, 4]])
        m2 = AP(lnst, 50, [[128, 128], [4, 4]]); v2 = AP(lnst, 51, [[128, 128], [4, 4]])
        N2()
        DVE.tensor_tensor(lnst[:, 64:68], m1, m2, ALU.subtract)
        DVE.tensor_tensor(lnst[:, 68:72], m1, m2, ALU.add)
        DVE.tensor_tensor(lnst[:, 76:80], v1, v2, ALU.add)
        DVE.tensor_tensor(lnst[:, 72:76], lnst[:, 64:68], lnst[:, 64:68], ALU.mult)
        DVE.tensor_scalar(lnst[:, 68:72], lnst[:, 68:72], 0.5, None, ALU.mult)
        DVE.tensor_scalar(lnst[:, 76:80], lnst[:, 76:80], 0.5, None, ALU.mult)
        DVE.tensor_scalar(lnst[:, 72:76], lnst[:, 72:76], 0.25, None, ALU.mult)
        N2()
        DVE.tensor_tensor(lnst[:, 80:84], lnst[:, 76:80], lnst[:, 72:76], ALU.add)
        N2()
        DVE.tensor_scalar(lnst[:, 84:88], lnst[:, 80:84], EPS, None, ALU.add)
        sDV.inc(DVE.engine_nop())
        ACT.wait_ge(sDV.h, sDV.n)
        sAC.inc(ACT.activation(lnst[:, 88:92], lnst[:, 84:88], AF.Sqrt))
        DVE.wait_ge(sAC.h, sAC.n)
        DVE.reciprocal(lnst[:, 92:96], lnst[:, 88:92])
        N2()
        DVE.tensor_tensor(lnst[:, 96:100], lnst[:, 88:92], lnst[:, 92:96], ALU.mult)
        N2()
        DVE.tensor_scalar(lnst[:, 96:100], lnst[:, 96:100], -1.0, 2.0, ALU.mult, ALU.add)
        N2()
        DVE.tensor_tensor(lnst[:, 100:104], lnst[:, 92:96], lnst[:, 96:100], ALU.mult)
        N2()
        DVE.tensor_tensor(lnst[:, 104:108], lnst[:, 68:72], lnst[:, 100:104], ALU.mult)
        N2()
        DVE.tensor_scalar(lnst[:, 104:108], lnst[:, 104:108], -1.0, None, ALU.mult)
        N2()
        for half2 in range(2):
            if idx >= 1:
                DVE.wait_ge(sXN.h, grp_st_done[-1][half2] if idx >= 1 else 0)
            for tl2 in range(2):
                tl = half2 * 2 + tl2
                DVE.tensor_scalar(xtok[half2][:, tl2 * 1024:(tl2 + 1) * 1024],
                                  xbuf[bi][:, tl * 1024:(tl + 1) * 1024],
                                  lnst[:, 100 + tl:101 + tl], lnst[:, 104 + tl:105 + tl],
                                  ALU.mult, ALU.add)
            sDV.inc(DVE.engine_nop())
            SP.wait_ge(sDV.h, sDV.n)
            sXN.inc(SP.dma_start(dst_t[4 * gi + 2 * half2], xtok[half2][:, 0:1024]), 16)
            sXN.inc(SP.dma_start(dst_t[4 * gi + 2 * half2 + 1], xtok[half2][:, 1024:2048]), 16)
            if half2 == 0:
                st_h0 = sXN.n
        grp_st_done.append((st_h0, sXN.n))
        grp_dve_done.append(sDV.n)

    SP.wait_ge(sXN.h, sXN.n)
    for ds in range(8):
        sTR.inc(SP.dma_start(xnT_own[:, ds * TOWN:(ds + 1) * TOWN],
                             xnown_hbm[:, ds * 128:(ds + 1) * 128], transpose=True), 16)
    trow_done = sTR.n
    sWT.inc(PL.dma_start(wkvS[:, 0:2048], wk.rearrange("(n p) m -> p n m", p=128)), 16)
    sWT.inc(PL.dma_start(wkvS[:, 2048:4096], wv.rearrange("(n p) m -> p n m", p=128)), 16)
    wkv_ld = sWT.n

    a1st = [nc.alloc_sbuf_tensor_at(f"a1st{i}", [128, 1024], BF16,
                                    offset=M["maskt"] + i * 2048) for i in range(2)]
    w1n_t = w1n.rearrange("(n p) m -> n p m", p=128)
    a1red = []
    for gi in range(32):
        bi = gi % 2
        if gi >= 2:
            PL.wait_ge(sDV.h, a1red[gi - 2])
        sWT.inc(PL.dma_start(a1st[bi][:, :], w1n_t[gi]), 16)
        DVE.wait_ge(sWT.h, sWT.n)
        DVE.reduce_sum(alpha1s[:, gi:gi + 1], a1st[bi][:, :],
                       axis=AXX, apply_absolute_value=True)
        sDV.inc(DVE.engine_nop())
        a1red.append(sDV.n)
    DVE.engine_nop(); DVE.engine_nop()
    DVE.tensor_scalar(alpha1s[:, :], alpha1s[:, :], 1.0 / 1024.0, None, ALU.mult)
    DVE.engine_nop(); DVE.engine_nop()
    sDV.inc(DVE.engine_nop())
    alpha1_done = sDV.n

    if debug and stage <= 1:
        SP.wait_ge(sTR.h, trow_done)
        sST.inc(SP.dma_start(dbg["d1"][:, 0:8 * TOWN], xnT_own[:, :]), 16)
        PL.wait_ge(sST.h, sST.n)
        return nc

    import contextlib
    PB1 = nc.alloc_psum_tensor("PB1", [128, 1024], F32)
    PB2 = nc.alloc_psum_tensor("PB2", [128, 1024], F32)
    PB3 = nc.alloc_psum_tensor("PB3", [128, 512], F32)
    PB4 = nc.alloc_psum_tensor("PB4", [128, 512], F32)
    PB5 = nc.alloc_psum_tensor("PB5", [128, 512], F32)
    PB6 = nc.alloc_psum_tensor("PB6", [128, 512], F32)
    pp = [PB1[:, 0:512], PB1[:, 512:1024]]
    pS = PB6[0:1, 0:512]
    pCb = PB6[:, 0:16]

    PE.wait_ge(sWT.h, wkv_ld)
    ACT.wait_ge(sDV.h, consts_done)
    q_done_pe = []
    kv_rope_done = []
    for qi in range(4):
        if qi > 0:
            SP.wait_ge(sPE.h, q_done_pe[qi - 1])
            SP.wait_ge(sSH.h, sSH.n)
        for ds in range(8):
            sTR.inc(SP.dma_start(xnT_q[:, ds * 1024:(ds + 1) * 1024],
                                 xn_hbm[qi * 1024:(qi + 1) * 1024, ds * 128:(ds + 1) * 128],
                                 transpose=True), 16)
        PE.wait_ge(sTR.h, sTR.n)
        if qi > 0:
            ACT.wait_ge(sDV.h, kv_rope_done[qi - 1])
        for mt in range(2):
            for ccq in range(2):
                bank = (mt * 2 + ccq) % 2
                PE.wait_ge(sAC.h, sAC.n)
                for dt in range(8):
                    mm = PE.matmul(pp[bank],
                              wkvS[:, dt * 256 + mt * 128: dt * 256 + mt * 128 + 128],
                              xnT_q[:, dt * 1024 + ccq * 512: dt * 1024 + ccq * 512 + 512],
                              start=(dt == 0), stop=(dt == 7))
                sPE.inc(mm)
                ACT.wait_ge(sPE.h, sPE.n)
                sAC.inc(ACT.activation(kstq[:, mt * 1024 + ccq * 512: mt * 1024 + ccq * 512 + 512],
                                       pp[bank], AF.Identity, bias=bk2s[:, mt:mt + 1]))
        for vt in range(8):
            tile = qi * 8 + vt
            bank = vt % 2
            PE.wait_ge(sAC.h, sAC.n)
            for dt in range(8):
                PE.matmul(pp[bank][:, 0:256],
                          xnT_q[:, dt * 1024 + vt * 128: dt * 1024 + vt * 128 + 128],
                          wkvS[:, 2048 + dt * 256: 2048 + (dt + 1) * 256],
                          start=(dt == 0), stop=False)
            mm = PE.matmul(pp[bank][:, 0:256], ones_l[0:1, 0:128], bvbs[0:1, :],
                      start=False, stop=True)
            sPE.inc(mm)
            ACT.wait_ge(sPE.h, sPE.n)
            sAC.inc(ACT.activation(
                AP(vp, tile * 272, [[32 * 272, 128], [68, 4], [1, 64]]),
                AP(PB1, bank * 512, [[1024, 128], [64, 4], [1, 64]]), AF.Copy))
        q_done_pe.append(sPE.n)
        DVE.wait_ge(sAC.h, sAC.n)
        cA = qi * 1024
        for mt in range(2):
            ks = kstq[:, mt * 1024:(mt + 1) * 1024]
            DVE.stream_shuffle(shbuf[:, :], ks, SWAP32)
            DVE.tensor_tensor(ks, ks, cks[:, cA:cA + 1024], ALU.mult)
            DVE.tensor_tensor(shbuf[:, :], shbuf[:, :], sks[:, cA:cA + 1024], ALU.mult)
            DVE.tensor_tensor(kt[2 * mt][0:64, cA:cA + 1024], ks[0:64, :], shbuf[0:64, :], ALU.add)
            DVE.tensor_tensor(ks[64:128, :], ks[64:128, :], shbuf[64:128, :], ALU.add)
            sDV.inc(DVE.engine_nop())
            SP.wait_ge(sDV.h, sDV.n)
            sSH.inc(SP.dma_start(kt[2 * mt + 1][0:64, cA:cA + 1024], ks[64:128, :]), 16)
        kv_rope_done.append(sDV.n)

    if debug and stage <= 2:
        DVE.wait_ge(sSH.h, sSH.n)
        SP.wait_ge(sDV.h, sDV.n)
        for g in range(G):
            sST.inc(SP.dma_start(dbg["d1"][0:64, g * T:(g + 1) * T], kt[g][0:64, :]), 16)
        sST.inc(SP.dma_start(dbg["d1"][100:101, 0:32 * 272], vp[0:1, :]), 16)
        PL.wait_ge(sST.h, sST.n)
        return nc

    PL.wait_ge(sPE.h, q_done_pe[3])
    sWT.inc(PL.dma_start(wqs[:, :], wq.rearrange("(n p) m -> p n m", p=128)), 16)
    wq_ld = sWT.n
    DVE.wait_ge(sSH.h, sSH.n)
    DVE.wait_ge(sPE.h, q_done_pe[3])
    for h in range(H):
        DVE.memset(qt[h][64:KP, :], 0.0)
    for g in range(G):
        DVE.memset(kt[g][64:KP, :], 0.0)
    sDV.inc(DVE.memset(q2st[64:KP, :], 0.0))
    pads_done = sDV.n

    PE.wait_ge(sWT.h, wq_ld)
    PE.wait_ge(sTR.h, trow_done)
    qrope_done = []
    for cc in range(4):
        for dqt in range(8):
            bank = dqt % 2
            PE.wait_ge(sAC.h, sAC.n)
            if cc > 0:
                PE.wait_ge(sDV.h, qrope_done[cc - 1])
            for dt in range(8):
                mm = PE.matmul(pp[bank],
                          wqs[:, dt * 1024 + dqt * 128: dt * 1024 + dqt * 128 + 128],
                          xnT_own[:, dt * TOWN + cc * 512: dt * TOWN + cc * 512 + 512],
                          start=(dt == 0), stop=(dt == 7))
            sPE.inc(mm)
            ACT.wait_ge(sPE.h, sPE.n)
            sAC.inc(ACT.activation(qstc[:, dqt * 512:(dqt + 1) * 512], pp[bank],
                                   AF.Identity, bias=bq2s[:, dqt:dqt + 1]))
        DVE.wait_ge(sAC.h, sAC.n)
        DVE.wait_ge(sDV.h, pads_done)
        co = cc * 512
        for dqt in range(8):
            qs = qstc[:, dqt * 512:(dqt + 1) * 512]
            DVE.stream_shuffle(shq[:, :], qs, SWAP32)
            DVE.tensor_tensor(qs, qs, cqs[:, co:co + 512], ALU.mult)
            DVE.tensor_tensor(shq[:, :], shq[:, :], sqs[:, co:co + 512], ALU.mult)
            DVE.tensor_tensor(qt[2 * dqt][0:64, co:co + 512], qs[0:64, :], shq[0:64, :], ALU.add)
            DVE.tensor_tensor(qs[64:128, :], qs[64:128, :], shq[64:128, :], ALU.add)
            sDV.inc(DVE.engine_nop())
            SP.wait_ge(sDV.h, sDV.n)
            sSH.inc(SP.dma_start(qt[2 * dqt + 1][0:64, co:co + 512], qs[64:128, :]), 16)
        sDV.inc(DVE.engine_nop())
        qrope_done.append(sDV.n)
    DVE.wait_ge(sSH.h, sSH.n)

    if debug and stage <= 3:
        sDV.inc(DVE.engine_nop())
        SP.wait_ge(sDV.h, sDV.n)
        SP.wait_ge(sSH.h, sSH.n)
        for h in range(H):
            sST.inc(SP.dma_start(dbg["d1"][0:KP, h * TOWN:(h + 1) * TOWN], qt[h][:, :]), 16)
        for g in range(G):
            sST.inc(SP.dma_start(dbg["d1"][KP:KP + 32, g * T:(g + 1) * T], kt[g][32:64, :]), 16)
        PL.wait_ge(sST.h, sST.n)
        return nc

    a_qk_dv = sDV.n

    pS4 = [PB6[0:1, 0:512], PB3[0:1, 0:512], PB4[0:1, 0:512], PB5[0:1, 0:512]]
    N2c = lambda: (DVE.engine_nop(), DVE.engine_nop())

    def max_of_sums(dst_col, init):
        PE.wait_ge(sDV.h, sDV.n)
        for ch in range(4):
            mm = PE.matmul(pS4[ch], ones_l[0:KP, 0:1], q2st[0:KP, ch * 512:(ch + 1) * 512],
                           start=True, stop=True)
        sPE.inc(mm)
        DVE.wait_ge(sPE.h, sPE.n)
        for ch in range(4):
            DVE.reduce_max(cbt[0:1, ch:ch + 1], pS4[ch], axis=AXX)
        N2c()
        DVE.tensor_tensor(cbt[0:1, 4:6], AP(cbt, 0, [[64, 1], [1, 2]]),
                          AP(cbt, 2, [[64, 1], [1, 2]]), ALU.max)
        N2c()
        DVE.tensor_tensor(cbt[0:1, 6:7], cbt[0:1, 4:5], cbt[0:1, 5:6], ALU.max)
        N2c()
        if init:
            DVE.tensor_copy(cbt[0:1, dst_col:dst_col + 1], cbt[0:1, 6:7])
        else:
            DVE.tensor_tensor(cbt[0:1, dst_col:dst_col + 1],
                              cbt[0:1, dst_col:dst_col + 1], cbt[0:1, 6:7], ALU.max)
        N2c()
        sDV.inc(DVE.engine_nop())

    for g in range(G):
        for hh in range(2):
            DVE.wait_ge(sPE.h, sPE.n)
            DVE.tensor_tensor(q2st[0:64, :], kt[g][0:64, hh * 2048:(hh + 1) * 2048],
                              kt[g][0:64, hh * 2048:(hh + 1) * 2048], ALU.mult)
            sDV.inc(DVE.engine_nop())
            max_of_sums(32 + g, hh == 0)
    for h in range(H):
        DVE.wait_ge(sPE.h, sPE.n)
        DVE.tensor_tensor(q2st[0:64, :], qt[h][0:64, :], qt[h][0:64, :], ALU.mult)
        sDV.inc(DVE.engine_nop())
        max_of_sums(36 + h, True)
    ACT.wait_ge(sDV.h, sDV.n)
    sAC.inc(ACT.activation(cbt[0:1, 32:52], cbt[0:1, 32:52], AF.Sqrt))
    DVE.wait_ge(sAC.h, sAC.n)
    DVE.tensor_tensor(cbt[0:1, 36:52], cbt[0:1, 36:52],
                      AP(cbt, 32, [[64, 1], [1, 4], [0, 4]]), ALU.mult)
    DVE.engine_nop(); DVE.engine_nop()
    DVE.tensor_scalar(cbt[0:1, 36:52], cbt[0:1, 36:52], -0.125, None, ALU.mult)
    DVE.engine_nop(); DVE.engine_nop()
    sDV.inc(DVE.tensor_copy(shq[0:1, 0:16], cbt[0:1, 36:52]))
    PE.wait_ge(sDV.h, sDV.n)
    mm = PE.matmul(pCb, ones_l[0:1, 0:128], shq[0:1, 0:16], start=True, stop=True)
    sPE.inc(mm)
    DVE.wait_ge(sPE.h, sPE.n)
    sDV.inc(DVE.tensor_copy(cbS[:, :], pCb))
    a_done_dv = sDV.n

    if debug and stage <= 4:
        SP.wait_ge(sDV.h, sDV.n)
        for h in range(H):
            sST.inc(SP.dma_start(dbg["d1"][0:KP, h * TOWN:(h + 1) * TOWN], qt[h][:, :]), 16)
        for g in range(G):
            sST.inc(SP.dma_start(dbg["d1"][KP:KP + 32, g * T:(g + 1) * T],
                                 kt[g][32:64, :]), 16)
        sST.inc(SP.dma_start(dbg["d2"][:, 0:16], cbS[:, :]), 16)
        sST.inc(SP.dma_start(dbg["d2"][1:2, 0:48], cbt[0:1, 16:64]), 16)
        PL.wait_ge(sST.h, sST.n)
        return nc

    PL.wait_ge(sPE.h, sPE.n)
    sWT.inc(PL.dma_start(mskb[:, :], mask4[:, :]), 16)

    pblk = [PB1, PB2]
    pv = [PB3[0:65, :], PB4[0:65, :], PB6[0:65, :]]
    prb = PB5

    PE.wait_ge(sDV.h, a_qk_dv)
    PE.wait_ge(sSH.h, sSH.n)
    ACT.wait_ge(sDV.h, a_done_dv)
    DVE.wait_ge(sWT.h, sWT.n)
    exp_done = [sAC.n, sAC.n]
    pt_free = [sPE.n, sPE.n]
    pv_free = [sDV.n, sDV.n, sDV.n]
    bidx = 0
    prev = None
    epi_state = {"prb_free": sDV.n, "astg_free": sST.n, "pv_done": 0,
                 "ac": 0, "dv": 0, "last_hilo": 0, "last_prb": 0}

    def epi_recip(hp):
        DVE.wait_ge(sPE.h, epi_state["pv_done"])
        DVE.tensor_copy(srow[64:65, 0:512], pv[hp][64:65, :])
        DVE.engine_nop(); DVE.engine_nop()
        DVE.reciprocal(rrow[64:65, 0:512], srow[64:65, 0:512])
        sDV.inc(DVE.engine_nop())

    def epi_hilo():
        DVE.wait_ge(sPE.h, epi_state["last_prb"])
        DVE.tensor_copy(rhl[64:65, 0:512], rrow[64:65, 0:512])
        DVE.engine_nop(); DVE.engine_nop()
        DVE.tensor_tensor(rhl[64:65, 512:1024], rrow[64:65, 0:512],
                          rhl[64:65, 0:512], ALU.subtract)
        sDV.inc(DVE.engine_nop())
        epi_state["dv"] = sDV.n

    def epi_partB(hp, oc, h):
        PE.wait_ge(sDV.h, epi_state["dv"])
        PE.wait_ge(sDV.h, epi_state["prb_free"])
        PE.matmul(prb[:, :], ones_l[64:65, 0:128], rhl[64:65, 0:512],
                  start=True, stop=False)
        mm = PE.matmul(prb[:, :], ones_l[64:65, 0:128], rhl[64:65, 512:1024],
                       start=False, stop=True)
        sPE.inc(mm)
        epi_state["last_prb"] = sPE.n
        DVE.wait_ge(sPE.h, sPE.n)
        DVE.wait_ge(sST.h, epi_state["astg_free"])
        DVE.tensor_copy(srow[0:64, 0:512], prb[0:64, :])
        DVE.engine_nop(); DVE.engine_nop()
        DVE.tensor_tensor(astg[0:64, :], pv[hp][0:64, :], srow[0:64, 0:512], ALU.mult)
        sDV.inc(DVE.engine_nop())
        pv_free[hp] = sDV.n
        epi_state["prb_free"] = sDV.n
        SP.wait_ge(sDV.h, sDV.n)
        sST.inc(SP.dma_start(attn_hbm[h * 64:(h + 1) * 64, oc:oc + 512],
                             astg[0:64, :]), 16)
        epi_state["astg_free"] = sST.n

    w2t_t = w2t.rearrange("(n p) m -> n p m", p=128)

    def load_s1(jh):
        sWT.inc(PL.dma_start(
            s1t[:, :], AP(w1t, jh * 2048, [[HID, 128], [128 * HID, 8], [1, 2048]])), 16)
        return sWT.n

    def load_s2(jh):
        for i in range(16):
            sWT.inc(PL.dma_start(s2t[:, i * 1024:(i + 1) * 1024], w2t_t[jh * 16 + i]), 16)
        return sWT.n

    it = 0
    for h in range(H):
        g = h // 4
        for ci in range(4):
            oc = ci * 512
            nb2 = 2 * NBRUN[ci]
            hp = it % 3

            def emit_exp(b):
                bb = (bidx + b) % 2
                ACT.wait_ge(sPE.h, s_done[b])
                ACT.wait_ge(sPE.h, pt_free[bb])
                sAC.inc(ACT.activation(pt[bb][:, 0:1024], pblk[bb][:, :], AF.Exp,
                                       bias=cbS[:, h:h + 1], scale=0.125))
                exp_done[bb] = sAC.n
                if b >= nb2 - 4:
                    s4 = (b - (nb2 - 4)) // 2
                    mo = ((ci % 2) * 2 + s4) * 2048 + (b % 2) * 1024
                    DVE.wait_ge(sAC.h, sAC.n)
                    DVE.tensor_tensor(pt[bb][:, 0:1024], pt[bb][:, 0:1024],
                                      mskb[:, mo:mo + 1024], ALU.mult)
                    sDV.inc(DVE.engine_nop())
                    p_ready[b] = ("dv", sDV.n)
                else:
                    p_ready[b] = ("ac", sAC.n)

            s_done = {}
            p_ready = {}
            for b in range(nb2):
                bb = (bidx + b) % 2
                if b == 0:
                    PE.wait_ge(sAC.h, exp_done[bb])
                else:
                    kind, val = p_ready[b - 1]
                    PE.wait_ge(sDV.h if kind == "dv" else sAC.h, val)
                    if b == 1:
                        PE.wait_ge(sDV.h, pv_free[hp])
                for j in range(2):
                    kt_i = 2 * b + j
                    mm = PE.matmul(pblk[bb][:, j * 512:(j + 1) * 512],
                                   kt[g][0:KP, kt_i * 128:(kt_i + 1) * 128],
                                   qt[h][0:KP, oc:oc + 512],
                                   start=True, stop=True)
                sPE.inc(mm)
                s_done[b] = sPE.n
                if b >= 1:
                    bbp = (bidx + b - 1) % 2
                    for j in range(2):
                        kt_i = 2 * (b - 1) + j
                        mm = PE.matmul(pv[hp],
                                       vp[:, kt_i * 272 + g * 68: kt_i * 272 + g * 68 + 65],
                                       pt[bbp][:, j * 512:(j + 1) * 512],
                                       start=(b == 1 and j == 0), stop=False)
                    sPE.inc(mm)
                    pt_free[bbp] = sPE.n
                emit_exp(b)
                if prev is not None:
                    if b == 1:
                        epi_recip(prev[0])
                    elif b == 2:
                        epi_hilo()
                    elif b == 3:
                        epi_partB(*prev)
                        prev = None
            kind, val = p_ready[nb2 - 1]
            PE.wait_ge(sDV.h if kind == "dv" else sAC.h, val)
            bbl = (bidx + nb2 - 1) % 2
            for j in range(2):
                kt_i = 2 * (nb2 - 1) + j
                mm = PE.matmul(pv[hp],
                               vp[:, kt_i * 272 + g * 68: kt_i * 272 + g * 68 + 65],
                               pt[bbl][:, j * 512:(j + 1) * 512],
                               start=False, stop=(j == 1))
            sPE.inc(mm)
            pt_free[bbl] = sPE.n
            prev = (hp, oc, h)
            epi_state["pv_done"] = sPE.n
            bidx += nb2
            it += 1
        if h == 7:
            PL.wait_ge(sPE.h, sPE.n)
            sWT.inc(PL.dma_start(wos[:, :], wo.rearrange("(n p) m -> p n m", p=128)), 16)
            wos_ld = sWT.n
            s1_ld = load_s1(0)
    if prev is not None:
        epi_recip(prev[0])
        epi_hilo()
        epi_partB(*prev)
        prev = None


    pco = [PB1, PB2]
    ph = [PB3[:, 0:256], PB4[:, 0:256]]
    pa2 = [PB5[0:1, 0:512], PB6[0:1, 0:512]]
    rcp = nc.alloc_sbuf_tensor_at("rcp", [1, D], F32, offset=M["_RA"] + 20480)
    b2pf = nc.alloc_sbuf_tensor_at("b2pf", [1, D], F32, offset=M["_RA"] + 24576)
    abufB = nc.alloc_sbuf_tensor_at("abufB", [128, 1024], BF16, offset=M["_RA"] + 12288)
    ab2 = [abuf, abufB]

    b_done_pe, b_done_dv, b_done_st = sPE.n, sDV.n, sST.n
    PL.wait_ge(sPE.h, b_done_pe)
    s2h1_ld = load_s2(1)

    ACT.wait_ge(sWT.h, s1_ld)
    ACT.wait_ge(sST.h, b_done_st)
    for ch in range(4):
        sAC.inc(ACT.activation(s1t[:, ch * 4096:(ch + 1) * 4096],
                               s1t[:, ch * 4096:(ch + 1) * 4096], AF.Sign))
    s1sign = {0: sAC.n}

    SP.wait_ge(sST.h, b_done_st)
    att_pe = {}
    a2mm_pe = {}
    oac_free = [sC2.n, sC2.n]
    pco_free = [sDV.n, sDV.n]
    for tt in range(16):
        bi = tt % 2
        ai = tt % 4
        if tt >= 4:
            SP.wait_ge(sPE.h, att_pe[tt - 4])
        sC1.inc(SP.dma_start(
            att[ai][:, :],
            AP(attn_hbm, tt * 128, [[TOWN, 128], [128 * TOWN, 8], [1, 128]])), 16)
        PE.wait_ge(sC1.h, sC1.n)
        if tt == 0:
            PE.wait_ge(sWT.h, wos_ld)
        PE.wait_ge(sDV.h, pco_free[bi])
        for nc_ in range(2):
            PE.matmul(pco[bi][:, nc_ * 512:(nc_ + 1) * 512], ones_l[0:1, 0:128],
                      bobs[0:1, nc_ * 512:(nc_ + 1) * 512], start=True, stop=False)
            for dd in range(8):
                mm = PE.matmul(pco[bi][:, nc_ * 512:(nc_ + 1) * 512],
                               att[ai][:, dd * 128:(dd + 1) * 128],
                               wos[:, dd * 1024 + nc_ * 512: dd * 1024 + nc_ * 512 + 512],
                               start=False, stop=(dd == 7))
        sPE.inc(mm)
        att_pe[tt] = sPE.n
        i = tt
        ACT.wait_ge(sWT.h, s2h1_ld - 16 * (15 - i))
        if i >= 2:
            ACT.wait_ge(sPE.h, a2mm_pe[i - 2])
        sAC.inc(ACT.activation(ab2[i % 2][:, :], s2t[:, i * 1024:(i + 1) * 1024], AF.Abs))
        PE.wait_ge(sAC.h, sAC.n)
        for nc_ in range(2):
            mm = PE.matmul(pa2[nc_], ones_l[:, 0:1],
                           ab2[i % 2][:, nc_ * 512:(nc_ + 1) * 512],
                           start=(i == 0), stop=False)
        sPE.inc(mm)
        a2mm_pe[i] = sPE.n
        DVE.wait_ge(sPE.h, att_pe[tt])
        DVE.wait_ge(sC2.h, oac_free[bi])
        DVE.tensor_copy(oac[bi][:, :], pco[bi][:, :])
        sDV.inc(DVE.engine_nop())
        pco_free[bi] = sDV.n
        SP.wait_ge(sDV.h, sDV.n)
        sC2.inc(SP.dma_start(oacc_t[tt], oac[bi][:, :]), 16)
        oac_free[bi] = sC2.n

    abs1_ac = sAC.n
    PL.wait_ge(sAC.h, abs1_ac)
    s2h0_ld = load_s2(0)
    a2mm0 = {}
    for i in range(16):
        ACT.wait_ge(sWT.h, s2h0_ld - 16 * (15 - i))
        if i >= 2:
            ACT.wait_ge(sPE.h, a2mm0[i - 2])
        sAC.inc(ACT.activation(ab2[i % 2][:, :], s2t[:, i * 1024:(i + 1) * 1024], AF.Abs))
        PE.wait_ge(sAC.h, sAC.n)
        for nc_ in range(2):
            mm = PE.matmul(pa2[nc_], ones_l[:, 0:1],
                           ab2[i % 2][:, nc_ * 512:(nc_ + 1) * 512],
                           start=False, stop=(i == 15))
        sPE.inc(mm)
        a2mm0[i] = sPE.n
    abs_pe = sPE.n
    for ch in range(4):
        sAC.inc(ACT.activation(s2t[:, ch * 4096:(ch + 1) * 4096],
                               s2t[:, ch * 4096:(ch + 1) * 4096], AF.Sign))
    s2sign = {0: sAC.n}

    DVE.wait_ge(sPE.h, abs_pe)
    DVE.tensor_copy(a2row[0:1, 0:512], pa2[0])
    DVE.tensor_copy(a2row[0:1, 512:1024], pa2[1])
    DVE.engine_nop(); DVE.engine_nop()
    DVE.tensor_scalar(a2row[0:1, :], a2row[0:1, :], 1.0 / 4096.0, None, ALU.mult)
    DVE.engine_nop(); DVE.engine_nop()
    DVE.reciprocal(rcp[0:1, :], a2row[0:1, :])
    DVE.engine_nop(); DVE.engine_nop()
    DVE.tensor_tensor(b2pf[0:1, :], b2rs[0:1, :], rcp[0:1, :], ALU.mult)
    DVE.engine_nop(); DVE.engine_nop()
    DVE.tensor_copy(b2pb[0:1, :], b2pf[0:1, :])
    DVE.tensor_copy(a2hl[0:1, 0:1024], a2row[0:1, :])
    DVE.engine_nop(); DVE.engine_nop()
    DVE.tensor_tensor(a2hl[0:1, 1024:2048], a2row[0:1, :], a2hl[0:1, 0:1024], ALU.subtract)
    sDV.inc(DVE.engine_nop())
    PE.wait_ge(sDV.h, sDV.n)
    PE.wait_ge(sDV.h, pco_free[0])
    for nc_ in range(2):
        PE.matmul(pco[0][:, nc_ * 512:(nc_ + 1) * 512], ones_l[0:1, 0:128],
                  a2hl[0:1, nc_ * 512:(nc_ + 1) * 512], start=True, stop=False)
        mm = PE.matmul(pco[0][:, nc_ * 512:(nc_ + 1) * 512], ones_l[0:1, 0:128],
                       a2hl[0:1, 1024 + nc_ * 512: 1024 + (nc_ + 1) * 512],
                       start=False, stop=True)
    sPE.inc(mm)
    DVE.wait_ge(sPE.h, sPE.n)
    DVE.tensor_copy(a2bc[:, :], pco[0][:, :])
    sDV.inc(DVE.engine_nop())
    pco_free[0] = sDV.n
    a2_done = sDV.n

    m1_end = {}

    def emit_ffn_half(jh):
        gelu_done = sAC.n
        for cc in range(8):
            for jt in range(16):
                bk = jt % 2
                if cc == 0 and jt == 0:
                    PE.wait_ge(sAC.h, s1sign[jh])
                    PE.wait_ge(sDV.h, b_done_dv)
                    ACT.wait_ge(sDV.h, alpha1_done)
                PE.wait_ge(sAC.h, gelu_done if jt < 2 else sAC.n - 1)
                for dt in range(8):
                    mm = PE.matmul(ph[bk],
                                   s1t[:, dt * 2048 + jt * 128: dt * 2048 + jt * 128 + 128],
                                   xnT_own[:, dt * TOWN + cc * 256: dt * TOWN + cc * 256 + 256],
                                   start=(dt == 0), stop=(dt == 7))
                sPE.inc(mm)
                ACT.wait_ge(sPE.h, sPE.n)
                sAC.inc(ACT.activation(hbuf[:, jt * 256:(jt + 1) * 256], ph[bk],
                                       AF.Gelu, bias=b12s[:, jh * 16 + jt: jh * 16 + jt + 1],
                                       scale=alpha1s[:, jh * 16 + jt: jh * 16 + jt + 1]))
            if cc == 7:
                m1_end[jh] = sPE.n
            gelu_done = sAC.n
            for tl in range(2):
                tt = cc * 2 + tl
                bi = tt % 2
                ACT.wait_ge(sDV.h, pco_free[bi])
                sC3.inc(ACT.dma_start(oac[bi][:, :], oacc_t[tt]), 16)
                if jh == 1:
                    sC3.inc(ACT.dma_start(xre[bi][:, :], x_own_t[tt]), 16)
                PE.wait_ge(sAC.h, gelu_done)
                PE.wait_ge(sDV.h, pco_free[bi])
                if cc == 0 and tl == 0:
                    PE.wait_ge(sAC.h, s2sign[jh])
                    if jh == 0:
                        PE.wait_ge(sDV.h, a2_done)
                for nc_ in range(2):
                    if jh == 0:
                        PE.matmul(pco[bi][:, nc_ * 512:(nc_ + 1) * 512], ones_l[0:1, 0:128],
                                  b2pb[0:1, nc_ * 512:(nc_ + 1) * 512], start=True, stop=False)
                    for jt in range(16):
                        mm = PE.matmul(pco[bi][:, nc_ * 512:(nc_ + 1) * 512],
                                       hbuf[:, jt * 256 + tl * 128: jt * 256 + tl * 128 + 128],
                                       s2t[:, jt * 1024 + nc_ * 512: jt * 1024 + nc_ * 512 + 512],
                                       start=(jh == 1 and jt == 0), stop=(jt == 15))
                sPE.inc(mm)
                DVE.wait_ge(sPE.h, sPE.n)
                DVE.wait_ge(sC3.h, sC3.n)
                DVE.wait_ge((sC2 if jh == 0 else sST).h, ost_free[bi])
                DVE.tensor_tensor(ost[bi][:, :], pco[bi][:, :], a2bc[:, :], ALU.mult)
                DVE.engine_nop(); DVE.engine_nop()
                DVE.tensor_tensor(ost[bi][:, :], ost[bi][:, :], oac[bi][:, :], ALU.add)
                if jh == 1:
                    DVE.engine_nop(); DVE.engine_nop()
                    DVE.tensor_tensor(ost[bi][:, :], ost[bi][:, :], xre[bi][:, :], ALU.add)
                sDV.inc(DVE.engine_nop())
                pco_free[bi] = sDV.n
                SP.wait_ge(sDV.h, sDV.n)
                if jh == 0:
                    sC2.inc(SP.dma_start(oacc_t[tt], ost[bi][:, :]), 16)
                    ost_free[bi] = sC2.n
                else:
                    sST.inc(SP.dma_start(y_t[tt], ost[bi][:, :]), 16)
                    ost_free[bi] = sST.n

    ost_free = [sC2.n, sC2.n]
    ACT.wait_ge(sC2.h, sC2.n)
    emit_ffn_half(0)
    PL.wait_ge(sPE.h, m1_end[0])
    s1_ld1 = load_s1(1)
    h0_pe = sPE.n
    PL.wait_ge(sPE.h, h0_pe)
    s2_ld1 = load_s2(1)
    ACT.wait_ge(sWT.h, s1_ld1)
    for ch in range(4):
        sAC.inc(ACT.activation(s1t[:, ch * 4096:(ch + 1) * 4096],
                               s1t[:, ch * 4096:(ch + 1) * 4096], AF.Sign))
    s1sign[1] = sAC.n
    ACT.wait_ge(sWT.h, s2_ld1)
    for ch in range(4):
        sAC.inc(ACT.activation(s2t[:, ch * 4096:(ch + 1) * 4096],
                               s2t[:, ch * 4096:(ch + 1) * 4096], AF.Sign))
    s2sign[1] = sAC.n
    ACT.wait_ge(sC2.h, sC2.n)
    emit_ffn_half(1)

    PL.wait_ge(sST.h, sST.n)
    return nc


def _rope_tables(cols):
    r = np.arange(128)
    i = (r % 64) // 2
    freq = np.exp(-(2.0 * i / 64.0) * np.log(np.float64(10000.0)))
    ang = freq[:, None] * cols[None, :].astype(np.float64)
    c = np.cos(ang)
    s = np.sin(ang) * np.where(r % 2 == 0, -1.0, 1.0)[:, None]
    out = []
    for v in (c, s):
        hi = v.astype(np.float32)
        lo = (v - hi.astype(np.float64)).astype(np.float32)
        out += [hi, lo]
    return out


def _masks(half):
    tk = np.arange(128)[:, None]
    tq = np.arange(512)[None, :]
    tri = np.zeros((128, 2048), dtype=np.float32)
    for j in range(4):
        tri[:, j * 512:(j + 1) * 512] = (tq >= tk + 128 * j).astype(np.float32)
    ones = np.ones((128, 2048), dtype=np.float32)
    zeros = np.zeros((128, 2048), dtype=np.float32)
    m = np.zeros((128, 4 * 2048), dtype=np.float32)
    for cp in range(2):
        a, bb = (tri, zeros) if (cp + half) % 2 == 0 else (ones, tri)
        m[:, (cp * 2) * 2048:(cp * 2 + 1) * 2048] = a
        m[:, (cp * 2 + 1) * 2048:(cp * 2 + 2) * 2048] = bb
    return m


def make_core_inputs(core, I):
    b, half = core // 2, core % 2
    x = np.asarray(I["L0_attn_x"][b], dtype=np.float32)
    rs = OWN_RANGES[half]
    x_own = np.concatenate([x[a:bb] for a, bb in rs], axis=0)
    own_cols = np.concatenate([np.arange(a, bb) for a, bb in rs])
    ch, cl, sh, sl = _rope_tables(np.arange(T))
    cqh_, cql_, sqh_, sql_ = _rope_tables(own_cols)
    f32 = lambda a: np.ascontiguousarray(a, dtype=np.float32)
    return {
        "x_all": f32(x),
        "x_own": f32(x_own),
        "wq": f32(I["L0_attn_Wq"]), "wk": f32(I["L0_attn_Wk"]),
        "wv": f32(I["L0_attn_Wv"]), "wo": f32(I["L0_attn_Wo"]),
        "w1t": f32(np.asarray(I["L0_ffn_w1"]).T),
        "w2t": f32(np.asarray(I["L0_ffn_w2"]).T),
        "w1n": f32(I["L0_ffn_w1"]),
        "bq2": f32(np.asarray(I["L0_attn_bq"]).reshape(8, 128).T),
        "bk2": f32(np.asarray(I["L0_attn_bk"]).reshape(2, 128).T),
        "b12": f32(np.asarray(I["L0_ffn_b1"]).reshape(32, 128).T),
        "bvr": f32(np.asarray(I["L0_attn_bv"]).reshape(1, 256)),
        "bor": f32(np.asarray(I["L0_attn_bo"]).reshape(1, D)),
        "b2r": f32(np.asarray(I["L0_ffn_b2"]).reshape(1, D)),
        "ckh": f32(ch), "ckl": f32(cl), "skh": f32(sh), "skl": f32(sl),
        "cqh": f32(cqh_), "cql": f32(cql_), "sqh": f32(sqh_), "sql": f32(sql_),
        "mask4": _masks(half),
    }


_NC_CACHE = {}


def kernel(**inputs):
    key = (_STAGE, _DEBUG)
    if key not in _NC_CACHE:
        _NC_CACHE[key] = build_program(stage=_STAGE, debug=_DEBUG)
    nc = _NC_CACHE[key]
    in_maps = [make_core_inputs(c, inputs) for c in range(8)]
    res = run_bass_kernel_spmd(nc, in_maps, core_ids=list(range(8)))
    out = np.zeros((4, T, D), dtype=np.float32)
    for c in range(8):
        b, half = c // 2, c % 2
        yv = np.asarray(res.results[c]["y"])
        o = 0
        for a, bb in OWN_RANGES[half]:
            out[b, a:bb] = yv[o:o + (bb - a)]
            o += bb - a
    return out


# version 09
# speedup vs baseline: 1.2555x; 1.2555x over previous
import numpy as np
import concourse.bass as bass
import concourse.mybir as mybir
from concourse.bass import AP
from concourse.bass_utils import run_bass_kernel_spmd

F32 = mybir.dt.float32
BF16 = mybir.dt.bfloat16
AF = mybir.ActivationFunctionType
ALU = mybir.AluOpType
AXX = mybir.AxisListType.X

D = 1024
T = 4096
H = 16
G = 4
HID = 4096
TOWN = 2048
KP = 96
EPS = 1e-5
CLAMP = float(2.0 ** 120)
SWAP32 = [i ^ 1 for i in range(32)]

OWN_CHUNKS = {0: [0, 3, 4, 7], 1: [1, 2, 5, 6]}
OWN_RANGES = {h: [(c * 512, (c + 1) * 512) for c in OWN_CHUNKS[h]] for h in (0, 1)}
QCHUNKS = {h: [c * 512 for c in OWN_CHUNKS[h]] for h in (0, 1)}
NBRUN = [2, 4, 6, 8]

_STAGE = 6
_DEBUG = False
_ATT_CH = 4
_ATT_H = 16
_ATT_FORCE_HALF = None


SBUF_BASE = 17024


def _mmap():
    m = {}
    cur = [SBUF_BASE]

    def add(name, nbytes, at=None):
        if at is None:
            at = cur[0]
            cur[0] += (nbytes + 31) & ~31
        m[name] = at
        return at

    add("xnT_own", 8 * TOWN * 2)
    add("vp", 32 * 272 * 2)
    add("qt", 16 * TOWN * 2)
    add("kt", 4 * T * 2)
    add("maskt", 4 * 512 * 2)
    add("bq2s", 8 * 4); add("bk2s", 2 * 4); add("b12s", 32 * 4)
    add("bvbs", 256 * 2); add("bobs", D * 2)
    add("b2rs", D * 4)
    add("alpha1s", 32 * 4)
    add("ones_l", 128 * 2)
    add("cb", 16 * 4)
    add("cbt", 64 * 4)
    add("cqs", TOWN * 2); add("sqs", TOWN * 2)
    ra = cur[0]
    m["_RA"] = ra
    add("xbuf0", 0, ra); add("xbuf1", 0, ra + 16384)
    add("xtok0", 0, m["qt"] + 16384); add("xtok1", 0, m["qt"] + 20480)
    add("lnst", 0, ra + 32768)
    add("xnT_q", 0, ra)
    add("kstq", 0, ra + 24576)
    add("shbuf", 0, ra + 28672)
    add("wqs", 0, ra)
    add("qstc", 0, ra + 16384)
    add("q2st", 0, ra + 24576)
    add("shq", 0, ra + 28672)
    add("pt0", 0, ra); add("pt1", 0, ra + 4096)
    add("astg", 0, ra + 8192)
    add("srow", 0, ra + 9216)
    add("rrow", 0, ra + 11264)
    add("rhl", 0, ra + 13312)
    add("mskb", 0, ra + 16384)
    q0 = m["qt"]
    add("cks", 0, q0); add("sks", 0, q0 + 8192)
    add("hilo0", 0, q0 + 16384); add("hilo1", 0, q0 + 32768)
    add("wkv", 0, q0 + 49152)
    add("s1t", 0, q0)
    add("s2t", 0, q0 + 32768)
    k0 = m["kt"]
    add("wos", 0, k0)
    add("hbuf", 0, k0 + 16384)
    add("xre0", 0, k0 + 24576); add("xre1", 0, k0 + 28672)
    v0 = m["vp"]
    add("oac0", 0, v0); add("oac1", 0, v0 + 4096)
    add("ost0", 0, v0 + 8192); add("ost1", 0, v0 + 12288)
    add("att0", 0, ra)
    add("att1", 0, ra + 2048)
    add("w1n0", 0, ra + 4096); add("w1n1", 0, ra + 6144)
    add("abuf", 0, ra + 8192)
    add("b2pb", 0, ra + 10240)
    add("a2hl", 0, ra + 12288)
    add("a2row", 0, ra + 16384)
    add("a2bc", 0, m["maskt"])
    m["_END"] = ra + 31744
    assert m["_END"] <= 229344
    assert m["_END"] <= 229344
    return m


class Sem:
    def __init__(self, nc, name):
        self.h = nc.alloc_semaphore(name=name)
        self.n = 0

    def inc(self, instr, k=1):
        self.n += k
        return instr.then_inc(self.h, k)


def build_program(stage=5, debug=False):
    nc = bass.Bass("TRN2", target_bir_lowering=False, detect_race_conditions=False)
    M = _mmap()

    def din(name, shape, dtype=F32):
        return nc.dram_tensor(name, list(shape), dtype, kind="ExternalInput")

    x_all = din("x_all", [T, D])
    x_own = din("x_own", [TOWN, D])
    wq = din("wq", [D, D]); wk = din("wk", [D, 256]); wv = din("wv", [D, 256])
    wo = din("wo", [D, D])
    w1t = din("w1t", [D, HID]); w2t = din("w2t", [HID, D]); w1n = din("w1n", [HID, D])
    bq2 = din("bq2", [128, 8]); bk2 = din("bk2", [128, 2]); b12 = din("b12", [128, 32])
    bvr = din("bvr", [1, 256]); bor = din("bor", [1, D]); b2r = din("b2r", [1, D])
    ckh = din("ckh", [128, T]); ckl = din("ckl", [128, T])
    skh = din("skh", [128, T]); skl = din("skl", [128, T])
    cqh = din("cqh", [128, TOWN]); cql = din("cql", [128, TOWN])
    sqh = din("sqh", [128, TOWN]); sql = din("sql", [128, TOWN])
    mask4 = din("mask4", [128, 4 * 2048])

    y = nc.dram_tensor("y", [TOWN, D], F32, kind="ExternalOutput")
    xn_hbm = nc.dram_tensor("xn_hbm", [T, D], BF16)
    xnown_hbm = nc.dram_tensor("xnown_hbm", [TOWN, D], BF16)
    attn_hbm = nc.dram_tensor("attn_hbm", [D, TOWN], BF16)
    oacc_hbm = nc.dram_tensor("oacc_hbm", [TOWN, D], F32)

    dbg = {}
    if debug:
        dbg["d1"] = nc.dram_tensor("dbg1", [128, 16 * TOWN], BF16, kind="ExternalOutput")
        dbg["d2"] = nc.dram_tensor("dbg2", [128, 64], F32, kind="ExternalOutput")

    def sbt(name, shape, dtype):
        return nc.alloc_sbuf_tensor_at(name, list(shape), dtype, offset=M[name])

    xnT_own = sbt("xnT_own", [128, 8 * TOWN], BF16)
    vp = sbt("vp", [128, 32 * 272], BF16)
    qt = [nc.alloc_sbuf_tensor_at(f"qt{h}", [KP, TOWN], BF16, offset=M["qt"] + h * TOWN * 2) for h in range(H)]
    kt = [nc.alloc_sbuf_tensor_at(f"kt{g}", [KP, T], BF16, offset=M["kt"] + g * T * 2) for g in range(G)]
    mskb = sbt("mskb", [128, 4 * 2048], BF16)
    bq2s = sbt("bq2s", [128, 8], F32); bk2s = sbt("bk2s", [128, 2], F32)
    b12s = sbt("b12s", [128, 32], F32)
    bvbs = sbt("bvbs", [1, 256], BF16); bobs = sbt("bobs", [1, D], BF16)
    b2rs = sbt("b2rs", [1, D], F32)
    alpha1s = sbt("alpha1s", [128, 32], F32)
    ones_l = sbt("ones_l", [128, 128], BF16)
    cbS = sbt("cb", [128, 16], F32)
    cbt = sbt("cbt", [1, 64], F32)
    cqs = sbt("cqs", [128, TOWN], BF16); sqs = sbt("sqs", [128, TOWN], BF16)
    xbuf = [sbt("xbuf0", [128, 4096], F32), sbt("xbuf1", [128, 4096], F32)]
    xtok = [sbt("xtok0", [128, 2048], BF16), sbt("xtok1", [128, 2048], BF16)]
    lnst = sbt("lnst", [128, 128], F32)
    xnT_q = sbt("xnT_q", [128, 8 * 1024], BF16)
    wkvS = sbt("wkv", [128, 4 * 1024], BF16)
    kstq = sbt("kstq", [128, 2 * 1024], BF16)
    shbuf = sbt("shbuf", [128, 1024], BF16)
    wqs = sbt("wqs", [128, 8 * 1024], BF16)
    qstc = sbt("qstc", [128, 8 * 512], BF16)
    q2st = sbt("q2st", [KP, TOWN], BF16)
    shq = sbt("shq", [128, 512], BF16)
    pt = [sbt("pt0", [128, 2048], BF16), sbt("pt1", [128, 2048], BF16)]
    astg = sbt("astg", [128, 512], BF16)
    srow = sbt("srow", [128, 512], F32)
    rrow = sbt("rrow", [128, 512], F32)
    rhl = sbt("rhl", [128, 1024], BF16)
    cks = sbt("cks", [128, T], BF16); sks = sbt("sks", [128, T], BF16)
    hilo = [sbt("hilo0", [128, T], F32), sbt("hilo1", [128, T], F32)]
    s1t = sbt("s1t", [128, 8 * 2048], BF16)
    s2t = sbt("s2t", [128, 16 * 1024], BF16)
    wos = sbt("wos", [128, 8 * 1024], BF16)
    hbuf = sbt("hbuf", [128, 16 * 256], BF16)
    xre = [sbt("xre0", [128, 1024], F32), sbt("xre1", [128, 1024], F32)]
    oac = [sbt("oac0", [128, 1024], F32), sbt("oac1", [128, 1024], F32)]
    ost = [sbt("ost0", [128, 1024], F32), sbt("ost1", [128, 1024], F32)]
    att = [sbt("att0", [128, 8 * 128], BF16), sbt("att1", [128, 8 * 128], BF16),
           nc.alloc_sbuf_tensor_at("att2", [128, 8 * 128], BF16, offset=M["w1n0"]),
           nc.alloc_sbuf_tensor_at("att3", [128, 8 * 128], BF16, offset=M["w1n1"])]
    w1nb = [sbt("w1n0", [128, 1024], BF16), sbt("w1n1", [128, 1024], BF16)]
    abuf = sbt("abuf", [128, 1024], BF16)
    b2pb = sbt("b2pb", [1, D], BF16)
    a2hl = sbt("a2hl", [1, 2048], BF16)
    a2row = sbt("a2row", [1, D], F32)
    a2bc = sbt("a2bc", [128, D], F32)

    SP, PE, ACT, DVE, PL = nc.sync, nc.tensor, nc.scalar, nc.vector, nc.gpsimd

    sLD = Sem(nc, "sLD"); sWT = Sem(nc, "sWT"); sXN = Sem(nc, "sXN")
    sTR = Sem(nc, "sTR"); sDV = Sem(nc, "sDV"); sAC = Sem(nc, "sAC")
    sPE = Sem(nc, "sPE"); sSH = Sem(nc, "sSH"); sST = Sem(nc, "sST")
    sC1 = Sem(nc, "sC1"); sC2 = Sem(nc, "sC2"); sC3 = Sem(nc, "sC3")

    x_all_t = x_all.rearrange("(n p) m -> n p m", p=128)
    x_own_t = x_own.rearrange("(n p) m -> n p m", p=128)
    xn_hbm_t = xn_hbm.rearrange("(n p) m -> n p m", p=128)
    xnown_hbm_t = xnown_hbm.rearrange("(n p) m -> n p m", p=128)
    oacc_t = oacc_hbm.rearrange("(n p) m -> n p m", p=128)
    y_t = y.rearrange("(n p) m -> n p m", p=128)

    for dst, src in ((bq2s, bq2), (bk2s, bk2), (b12s, b12), (b2rs, b2r)):
        sLD.inc(SP.dma_start(dst[:, :], src[:, :]), 16)
    sWT.inc(PL.dma_start(bvbs[:, :], bvr[:, :]), 16)
    sWT.inc(PL.dma_start(bobs[:, :], bor[:, :]), 16)

    for ti, (hsrc, lsrc, dstb) in enumerate(((ckh, ckl, cks), (skh, skl, sks),
                                             (cqh, cql, cqs), (sqh, sql, sqs))):
        n = hsrc.shape[1]
        if ti > 0:
            SP.wait_ge(sDV.h, sDV.n)
        sLD.inc(SP.dma_start(hilo[0][:, 0:n], hsrc[:, :]), 16)
        sLD.inc(SP.dma_start(hilo[1][:, 0:n], lsrc[:, :]), 16)
        DVE.wait_ge(sLD.h, sLD.n)
        DVE.tensor_tensor(hilo[0][:, 0:n], hilo[0][:, 0:n], hilo[1][:, 0:n], ALU.add)
        sDV.inc(DVE.tensor_copy(dstb[:, 0:n], hilo[0][:, 0:n]))
    DVE.memset(ones_l[:, :], 1.0)
    sDV.inc(DVE.memset(vp[:, :], 1.0))
    consts_done = sDV.n

    ln_groups = [("all", gi) for gi in range(8)] + [("own", gi) for gi in range(4)]
    grp_dve_done = []
    grp_st_done = []
    N2 = lambda: (DVE.engine_nop(), DVE.engine_nop())
    for idx, (kind, gi) in enumerate(ln_groups):
        bi = idx % 2
        src_t = x_all_t if kind == "all" else x_own_t
        dst_t = xn_hbm_t if kind == "all" else xnown_hbm_t
        if idx >= 2:
            SP.wait_ge(sDV.h, grp_dve_done[idx - 2])
        for tl in range(4):
            sLD.inc(SP.dma_start(xbuf[bi][:, tl * 1024:(tl + 1) * 1024], src_t[4 * gi + tl]), 16)
        DVE.wait_ge(sLD.h, sLD.n)
        for tl in range(4):
            DVE.bn_stats(lnst[:, 12 * tl:12 * tl + 6], xbuf[bi][:, tl * 1024:tl * 1024 + 512])
            DVE.bn_stats(lnst[:, 12 * tl + 6:12 * tl + 12], xbuf[bi][:, tl * 1024 + 512:tl * 1024 + 1024])
            DVE.bn_aggr(lnst[:, 48 + 4 * tl:50 + 4 * tl], lnst[:, 12 * tl:12 * tl + 6])
            DVE.bn_aggr(lnst[:, 50 + 4 * tl:52 + 4 * tl], lnst[:, 12 * tl + 6:12 * tl + 12])
        m1 = AP(lnst, 48, [[128, 128], [4, 4]]); v1 = AP(lnst, 49, [[128, 128], [4, 4]])
        m2 = AP(lnst, 50, [[128, 128], [4, 4]]); v2 = AP(lnst, 51, [[128, 128], [4, 4]])
        N2()
        DVE.tensor_tensor(lnst[:, 64:68], m1, m2, ALU.subtract)
        DVE.tensor_tensor(lnst[:, 68:72], m1, m2, ALU.add)
        DVE.tensor_tensor(lnst[:, 76:80], v1, v2, ALU.add)
        DVE.tensor_tensor(lnst[:, 72:76], lnst[:, 64:68], lnst[:, 64:68], ALU.mult)
        DVE.tensor_scalar(lnst[:, 68:72], lnst[:, 68:72], 0.5, None, ALU.mult)
        DVE.tensor_scalar(lnst[:, 76:80], lnst[:, 76:80], 0.5, None, ALU.mult)
        DVE.tensor_scalar(lnst[:, 72:76], lnst[:, 72:76], 0.25, None, ALU.mult)
        N2()
        DVE.tensor_tensor(lnst[:, 80:84], lnst[:, 76:80], lnst[:, 72:76], ALU.add)
        N2()
        DVE.tensor_scalar(lnst[:, 84:88], lnst[:, 80:84], EPS, None, ALU.add)
        sDV.inc(DVE.engine_nop())
        ACT.wait_ge(sDV.h, sDV.n)
        sAC.inc(ACT.activation(lnst[:, 88:92], lnst[:, 84:88], AF.Sqrt))
        DVE.wait_ge(sAC.h, sAC.n)
        DVE.reciprocal(lnst[:, 92:96], lnst[:, 88:92])
        N2()
        DVE.tensor_tensor(lnst[:, 96:100], lnst[:, 88:92], lnst[:, 92:96], ALU.mult)
        N2()
        DVE.tensor_scalar(lnst[:, 96:100], lnst[:, 96:100], -1.0, 2.0, ALU.mult, ALU.add)
        N2()
        DVE.tensor_tensor(lnst[:, 100:104], lnst[:, 92:96], lnst[:, 96:100], ALU.mult)
        N2()
        DVE.tensor_tensor(lnst[:, 104:108], lnst[:, 68:72], lnst[:, 100:104], ALU.mult)
        N2()
        DVE.tensor_scalar(lnst[:, 104:108], lnst[:, 104:108], -1.0, None, ALU.mult)
        N2()
        for half2 in range(2):
            if idx >= 1:
                DVE.wait_ge(sXN.h, grp_st_done[-1][half2] if idx >= 1 else 0)
            for tl2 in range(2):
                tl = half2 * 2 + tl2
                DVE.tensor_scalar(xtok[half2][:, tl2 * 1024:(tl2 + 1) * 1024],
                                  xbuf[bi][:, tl * 1024:(tl + 1) * 1024],
                                  lnst[:, 100 + tl:101 + tl], lnst[:, 104 + tl:105 + tl],
                                  ALU.mult, ALU.add)
            sDV.inc(DVE.engine_nop())
            SP.wait_ge(sDV.h, sDV.n)
            sXN.inc(SP.dma_start(dst_t[4 * gi + 2 * half2], xtok[half2][:, 0:1024]), 16)
            sXN.inc(SP.dma_start(dst_t[4 * gi + 2 * half2 + 1], xtok[half2][:, 1024:2048]), 16)
            if half2 == 0:
                st_h0 = sXN.n
        grp_st_done.append((st_h0, sXN.n))
        grp_dve_done.append(sDV.n)

    SP.wait_ge(sXN.h, sXN.n)
    for ds in range(8):
        sTR.inc(SP.dma_start(xnT_own[:, ds * TOWN:(ds + 1) * TOWN],
                             xnown_hbm[:, ds * 128:(ds + 1) * 128], transpose=True), 16)
    trow_done = sTR.n
    sWT.inc(PL.dma_start(wkvS[:, 0:2048], wk.rearrange("(n p) m -> p n m", p=128)), 16)
    sWT.inc(PL.dma_start(wkvS[:, 2048:4096], wv.rearrange("(n p) m -> p n m", p=128)), 16)
    wkv_ld = sWT.n

    a1st = [nc.alloc_sbuf_tensor_at(f"a1st{i}", [128, 1024], BF16,
                                    offset=M["maskt"] + i * 2048) for i in range(2)]
    w1n_t = w1n.rearrange("(n p) m -> n p m", p=128)
    a1red = []
    for gi in range(32):
        bi = gi % 2
        if gi >= 2:
            PL.wait_ge(sDV.h, a1red[gi - 2])
        sWT.inc(PL.dma_start(a1st[bi][:, :], w1n_t[gi]), 16)
        DVE.wait_ge(sWT.h, sWT.n)
        DVE.reduce_sum(alpha1s[:, gi:gi + 1], a1st[bi][:, :],
                       axis=AXX, apply_absolute_value=True)
        sDV.inc(DVE.engine_nop())
        a1red.append(sDV.n)
    DVE.engine_nop(); DVE.engine_nop()
    DVE.tensor_scalar(alpha1s[:, :], alpha1s[:, :], 1.0 / 1024.0, None, ALU.mult)
    DVE.engine_nop(); DVE.engine_nop()
    sDV.inc(DVE.engine_nop())
    alpha1_done = sDV.n

    if debug and stage <= 1:
        SP.wait_ge(sTR.h, trow_done)
        sST.inc(SP.dma_start(dbg["d1"][:, 0:8 * TOWN], xnT_own[:, :]), 16)
        PL.wait_ge(sST.h, sST.n)
        return nc

    import contextlib
    PB1 = nc.alloc_psum_tensor("PB1", [128, 1024], F32)
    PB2 = nc.alloc_psum_tensor("PB2", [128, 1024], F32)
    PB3 = nc.alloc_psum_tensor("PB3", [128, 512], F32)
    PB4 = nc.alloc_psum_tensor("PB4", [128, 512], F32)
    PB5 = nc.alloc_psum_tensor("PB5", [128, 512], F32)
    PB6 = nc.alloc_psum_tensor("PB6", [128, 512], F32)
    pp = [PB1[:, 0:512], PB1[:, 512:1024]]
    pS = PB6[0:1, 0:512]
    pCb = PB6[:, 0:16]

    PE.wait_ge(sWT.h, wkv_ld)
    ACT.wait_ge(sDV.h, consts_done)
    q_done_pe = []
    kv_rope_done = []
    for qi in range(4):
        if qi > 0:
            SP.wait_ge(sPE.h, q_done_pe[qi - 1])
            SP.wait_ge(sSH.h, sSH.n)
        for ds in range(8):
            sTR.inc(SP.dma_start(xnT_q[:, ds * 1024:(ds + 1) * 1024],
                                 xn_hbm[qi * 1024:(qi + 1) * 1024, ds * 128:(ds + 1) * 128],
                                 transpose=True), 16)
        PE.wait_ge(sTR.h, sTR.n)
        if qi > 0:
            ACT.wait_ge(sDV.h, kv_rope_done[qi - 1])
        for mt in range(2):
            for ccq in range(2):
                bank = (mt * 2 + ccq) % 2
                PE.wait_ge(sAC.h, sAC.n)
                for dt in range(8):
                    mm = PE.matmul(pp[bank],
                              wkvS[:, dt * 256 + mt * 128: dt * 256 + mt * 128 + 128],
                              xnT_q[:, dt * 1024 + ccq * 512: dt * 1024 + ccq * 512 + 512],
                              start=(dt == 0), stop=(dt == 7))
                sPE.inc(mm)
                ACT.wait_ge(sPE.h, sPE.n)
                sAC.inc(ACT.activation(kstq[:, mt * 1024 + ccq * 512: mt * 1024 + ccq * 512 + 512],
                                       pp[bank], AF.Identity, bias=bk2s[:, mt:mt + 1]))
        for vt in range(8):
            tile = qi * 8 + vt
            bank = vt % 2
            PE.wait_ge(sAC.h, sAC.n)
            for dt in range(8):
                PE.matmul(pp[bank][:, 0:256],
                          xnT_q[:, dt * 1024 + vt * 128: dt * 1024 + vt * 128 + 128],
                          wkvS[:, 2048 + dt * 256: 2048 + (dt + 1) * 256],
                          start=(dt == 0), stop=False)
            mm = PE.matmul(pp[bank][:, 0:256], ones_l[0:1, 0:128], bvbs[0:1, :],
                      start=False, stop=True)
            sPE.inc(mm)
            ACT.wait_ge(sPE.h, sPE.n)
            sAC.inc(ACT.activation(
                AP(vp, tile * 272, [[32 * 272, 128], [68, 4], [1, 64]]),
                AP(PB1, bank * 512, [[1024, 128], [64, 4], [1, 64]]), AF.Copy))
        q_done_pe.append(sPE.n)
        DVE.wait_ge(sAC.h, sAC.n)
        cA = qi * 1024
        for mt in range(2):
            ks = kstq[:, mt * 1024:(mt + 1) * 1024]
            DVE.stream_shuffle(shbuf[:, :], ks, SWAP32)
            DVE.tensor_tensor(ks, ks, cks[:, cA:cA + 1024], ALU.mult)
            DVE.tensor_tensor(shbuf[:, :], shbuf[:, :], sks[:, cA:cA + 1024], ALU.mult)
            DVE.tensor_tensor(kt[2 * mt][0:64, cA:cA + 1024], ks[0:64, :], shbuf[0:64, :], ALU.add)
            DVE.tensor_tensor(ks[64:128, :], ks[64:128, :], shbuf[64:128, :], ALU.add)
            sDV.inc(DVE.engine_nop())
            SP.wait_ge(sDV.h, sDV.n)
            sSH.inc(SP.dma_start(kt[2 * mt + 1][0:64, cA:cA + 1024], ks[64:128, :]), 16)
        kv_rope_done.append(sDV.n)

    if debug and stage <= 2:
        DVE.wait_ge(sSH.h, sSH.n)
        SP.wait_ge(sDV.h, sDV.n)
        for g in range(G):
            sST.inc(SP.dma_start(dbg["d1"][0:64, g * T:(g + 1) * T], kt[g][0:64, :]), 16)
        sST.inc(SP.dma_start(dbg["d1"][100:101, 0:32 * 272], vp[0:1, :]), 16)
        PL.wait_ge(sST.h, sST.n)
        return nc

    PL.wait_ge(sPE.h, q_done_pe[3])
    sWT.inc(PL.dma_start(wqs[:, :], wq.rearrange("(n p) m -> p n m", p=128)), 16)
    wq_ld = sWT.n
    DVE.wait_ge(sSH.h, sSH.n)
    DVE.wait_ge(sPE.h, q_done_pe[3])
    for h in range(H):
        DVE.memset(qt[h][64:KP, :], 0.0)
    for g in range(G):
        DVE.memset(kt[g][64:KP, :], 0.0)
    sDV.inc(DVE.memset(q2st[64:KP, :], 0.0))
    pads_done = sDV.n

    PE.wait_ge(sWT.h, wq_ld)
    PE.wait_ge(sTR.h, trow_done)
    qrope_done = []
    for cc in range(4):
        for dqt in range(8):
            bank = dqt % 2
            PE.wait_ge(sAC.h, sAC.n)
            if cc > 0:
                PE.wait_ge(sDV.h, qrope_done[cc - 1])
            for dt in range(8):
                mm = PE.matmul(pp[bank],
                          wqs[:, dt * 1024 + dqt * 128: dt * 1024 + dqt * 128 + 128],
                          xnT_own[:, dt * TOWN + cc * 512: dt * TOWN + cc * 512 + 512],
                          start=(dt == 0), stop=(dt == 7))
            sPE.inc(mm)
            ACT.wait_ge(sPE.h, sPE.n)
            sAC.inc(ACT.activation(qstc[:, dqt * 512:(dqt + 1) * 512], pp[bank],
                                   AF.Identity, bias=bq2s[:, dqt:dqt + 1]))
        DVE.wait_ge(sAC.h, sAC.n)
        DVE.wait_ge(sDV.h, pads_done)
        co = cc * 512
        for dqt in range(8):
            qs = qstc[:, dqt * 512:(dqt + 1) * 512]
            DVE.stream_shuffle(shq[:, :], qs, SWAP32)
            DVE.tensor_tensor(qs, qs, cqs[:, co:co + 512], ALU.mult)
            DVE.tensor_tensor(shq[:, :], shq[:, :], sqs[:, co:co + 512], ALU.mult)
            DVE.tensor_tensor(qt[2 * dqt][0:64, co:co + 512], qs[0:64, :], shq[0:64, :], ALU.add)
            DVE.tensor_tensor(qs[64:128, :], qs[64:128, :], shq[64:128, :], ALU.add)
            sDV.inc(DVE.engine_nop())
            SP.wait_ge(sDV.h, sDV.n)
            sSH.inc(SP.dma_start(qt[2 * dqt + 1][0:64, co:co + 512], qs[64:128, :]), 16)
        sDV.inc(DVE.engine_nop())
        qrope_done.append(sDV.n)
    DVE.wait_ge(sSH.h, sSH.n)

    if debug and stage <= 3:
        sDV.inc(DVE.engine_nop())
        SP.wait_ge(sDV.h, sDV.n)
        SP.wait_ge(sSH.h, sSH.n)
        for h in range(H):
            sST.inc(SP.dma_start(dbg["d1"][0:KP, h * TOWN:(h + 1) * TOWN], qt[h][:, :]), 16)
        for g in range(G):
            sST.inc(SP.dma_start(dbg["d1"][KP:KP + 32, g * T:(g + 1) * T], kt[g][32:64, :]), 16)
        PL.wait_ge(sST.h, sST.n)
        return nc

    a_qk_dv = sDV.n

    pS4 = [PB6[0:1, 0:512], PB3[0:1, 0:512], PB4[0:1, 0:512], PB5[0:1, 0:512]]
    N2c = lambda: (DVE.engine_nop(), DVE.engine_nop())

    def max_of_sums(dst_col, init):
        PE.wait_ge(sDV.h, sDV.n)
        for ch in range(4):
            mm = PE.matmul(pS4[ch], ones_l[0:KP, 0:1], q2st[0:KP, ch * 512:(ch + 1) * 512],
                           start=True, stop=True)
        sPE.inc(mm)
        DVE.wait_ge(sPE.h, sPE.n)
        for ch in range(4):
            DVE.reduce_max(cbt[0:1, ch:ch + 1], pS4[ch], axis=AXX)
        N2c()
        DVE.tensor_tensor(cbt[0:1, 4:6], AP(cbt, 0, [[64, 1], [1, 2]]),
                          AP(cbt, 2, [[64, 1], [1, 2]]), ALU.max)
        N2c()
        DVE.tensor_tensor(cbt[0:1, 6:7], cbt[0:1, 4:5], cbt[0:1, 5:6], ALU.max)
        N2c()
        if init:
            DVE.tensor_copy(cbt[0:1, dst_col:dst_col + 1], cbt[0:1, 6:7])
        else:
            DVE.tensor_tensor(cbt[0:1, dst_col:dst_col + 1],
                              cbt[0:1, dst_col:dst_col + 1], cbt[0:1, 6:7], ALU.max)
        N2c()
        sDV.inc(DVE.engine_nop())

    for g in range(G):
        for hh in range(2):
            DVE.wait_ge(sPE.h, sPE.n)
            DVE.tensor_tensor(q2st[0:64, :], kt[g][0:64, hh * 2048:(hh + 1) * 2048],
                              kt[g][0:64, hh * 2048:(hh + 1) * 2048], ALU.mult)
            sDV.inc(DVE.engine_nop())
            max_of_sums(32 + g, hh == 0)
    for h in range(H):
        DVE.wait_ge(sPE.h, sPE.n)
        DVE.tensor_tensor(q2st[0:64, :], qt[h][0:64, :], qt[h][0:64, :], ALU.mult)
        sDV.inc(DVE.engine_nop())
        max_of_sums(36 + h, True)
    ACT.wait_ge(sDV.h, sDV.n)
    sAC.inc(ACT.activation(cbt[0:1, 32:52], cbt[0:1, 32:52], AF.Sqrt))
    DVE.wait_ge(sAC.h, sAC.n)
    DVE.tensor_tensor(cbt[0:1, 36:52], cbt[0:1, 36:52],
                      AP(cbt, 32, [[64, 1], [1, 4], [0, 4]]), ALU.mult)
    DVE.engine_nop(); DVE.engine_nop()
    DVE.tensor_scalar(cbt[0:1, 36:52], cbt[0:1, 36:52], -0.125, None, ALU.mult)
    DVE.engine_nop(); DVE.engine_nop()
    sDV.inc(DVE.tensor_copy(shq[0:1, 0:16], cbt[0:1, 36:52]))
    PE.wait_ge(sDV.h, sDV.n)
    mm = PE.matmul(pCb, ones_l[0:1, 0:128], shq[0:1, 0:16], start=True, stop=True)
    sPE.inc(mm)
    DVE.wait_ge(sPE.h, sPE.n)
    sDV.inc(DVE.tensor_copy(cbS[:, :], pCb))
    a_done_dv = sDV.n

    if debug and stage <= 4:
        SP.wait_ge(sDV.h, sDV.n)
        for h in range(H):
            sST.inc(SP.dma_start(dbg["d1"][0:KP, h * TOWN:(h + 1) * TOWN], qt[h][:, :]), 16)
        for g in range(G):
            sST.inc(SP.dma_start(dbg["d1"][KP:KP + 32, g * T:(g + 1) * T],
                                 kt[g][32:64, :]), 16)
        sST.inc(SP.dma_start(dbg["d2"][:, 0:16], cbS[:, :]), 16)
        sST.inc(SP.dma_start(dbg["d2"][1:2, 0:48], cbt[0:1, 16:64]), 16)
        PL.wait_ge(sST.h, sST.n)
        return nc

    PL.wait_ge(sPE.h, sPE.n)
    sWT.inc(PL.dma_start(mskb[:, :], mask4[:, :]), 16)

    pblk = [PB1, PB2]
    pv = [PB3[0:65, :], PB4[0:65, :], PB6[0:65, :]]
    prb = PB5

    PE.wait_ge(sDV.h, a_qk_dv)
    PE.wait_ge(sSH.h, sSH.n)
    ACT.wait_ge(sDV.h, a_done_dv)
    DVE.wait_ge(sWT.h, sWT.n)
    exp_done = [sAC.n, sAC.n]
    pt_free = [sPE.n, sPE.n]
    pv_free = [sDV.n, sDV.n, sDV.n]
    bidx = 0
    prev = None
    epi_state = {"prb_free": sDV.n, "astg_free": sST.n, "pv_done": 0,
                 "ac": 0, "dv": 0, "last_hilo": 0, "last_prb": 0}

    def epi_recip(hp):
        DVE.wait_ge(sPE.h, epi_state["pv_done"])
        DVE.tensor_copy(srow[64:65, 0:512], pv[hp][64:65, :])
        DVE.engine_nop(); DVE.engine_nop()
        DVE.reciprocal(rrow[64:65, 0:512], srow[64:65, 0:512])
        sDV.inc(DVE.engine_nop())

    def epi_hilo():
        DVE.wait_ge(sPE.h, epi_state["last_prb"])
        DVE.tensor_copy(rhl[64:65, 0:512], rrow[64:65, 0:512])
        DVE.engine_nop(); DVE.engine_nop()
        DVE.tensor_tensor(rhl[64:65, 512:1024], rrow[64:65, 0:512],
                          rhl[64:65, 0:512], ALU.subtract)
        sDV.inc(DVE.engine_nop())
        epi_state["dv"] = sDV.n

    def epi_partB(hp, oc, h):
        PE.wait_ge(sDV.h, epi_state["dv"])
        PE.wait_ge(sDV.h, epi_state["prb_free"])
        PE.matmul(prb[:, :], ones_l[64:65, 0:128], rhl[64:65, 0:512],
                  start=True, stop=False)
        mm = PE.matmul(prb[:, :], ones_l[64:65, 0:128], rhl[64:65, 512:1024],
                       start=False, stop=True)
        sPE.inc(mm)
        epi_state["last_prb"] = sPE.n
        DVE.wait_ge(sPE.h, sPE.n)
        DVE.wait_ge(sST.h, epi_state["astg_free"])
        DVE.tensor_copy(srow[0:64, 0:512], prb[0:64, :])
        DVE.engine_nop(); DVE.engine_nop()
        DVE.tensor_tensor(astg[0:64, :], pv[hp][0:64, :], srow[0:64, 0:512], ALU.mult)
        sDV.inc(DVE.engine_nop())
        pv_free[hp] = sDV.n
        epi_state["prb_free"] = sDV.n
        SP.wait_ge(sDV.h, sDV.n)
        sST.inc(SP.dma_start(attn_hbm[h * 64:(h + 1) * 64, oc:oc + 512],
                             astg[0:64, :]), 16)
        epi_state["astg_free"] = sST.n

    w2t_t = w2t.rearrange("(n p) m -> n p m", p=128)

    def load_s1(jh):
        sWT.inc(PL.dma_start(
            s1t[:, :], AP(w1t, jh * 2048, [[HID, 128], [128 * HID, 8], [1, 2048]])), 16)
        return sWT.n

    def load_s2(jh):
        for i in range(16):
            sWT.inc(PL.dma_start(s2t[:, i * 1024:(i + 1) * 1024], w2t_t[jh * 16 + i]), 16)
        return sWT.n

    it = 0
    for h in range(H):
        g = h // 4
        for ci in range(4):
            oc = ci * 512
            nb2 = 2 * NBRUN[ci]
            hp = it % 3

            def emit_exp(b):
                bb = (bidx + b) % 2
                ACT.wait_ge(sPE.h, s_done[b])
                ACT.wait_ge(sPE.h, pt_free[bb])
                sAC.inc(ACT.activation(pt[bb][:, 0:1024], pblk[bb][:, :], AF.Exp,
                                       bias=cbS[:, h:h + 1], scale=0.125))
                exp_done[bb] = sAC.n
                if b >= nb2 - 4:
                    s4 = (b - (nb2 - 4)) // 2
                    mo = ((ci % 2) * 2 + s4) * 2048 + (b % 2) * 1024
                    DVE.wait_ge(sAC.h, sAC.n)
                    DVE.tensor_tensor(pt[bb][:, 0:1024], pt[bb][:, 0:1024],
                                      mskb[:, mo:mo + 1024], ALU.mult)
                    sDV.inc(DVE.engine_nop())
                    p_ready[b] = ("dv", sDV.n)
                else:
                    p_ready[b] = ("ac", sAC.n)

            s_done = {}
            p_ready = {}
            for b in range(nb2):
                bb = (bidx + b) % 2
                PE.wait_ge(sAC.h, exp_done[bb])
                for j in range(2):
                    kt_i = 2 * b + j
                    mm = PE.matmul(pblk[bb][:, j * 512:(j + 1) * 512],
                                   kt[g][0:KP, kt_i * 128:(kt_i + 1) * 128],
                                   qt[h][0:KP, oc:oc + 512],
                                   start=True, stop=True)
                sPE.inc(mm)
                s_done[b] = sPE.n
                if b >= 1:
                    kind, val = p_ready[b - 1]
                    PE.wait_ge(sDV.h if kind == "dv" else sAC.h, val)
                    if b == 1:
                        PE.wait_ge(sDV.h, pv_free[hp])
                    bbp = (bidx + b - 1) % 2
                    for j in range(2):
                        kt_i = 2 * (b - 1) + j
                        mm = PE.matmul(pv[hp],
                                       vp[:, kt_i * 272 + g * 68: kt_i * 272 + g * 68 + 65],
                                       pt[bbp][:, j * 512:(j + 1) * 512],
                                       start=(b == 1 and j == 0), stop=False)
                    sPE.inc(mm)
                    pt_free[bbp] = sPE.n
                emit_exp(b)
                if prev is not None:
                    if b == 1:
                        epi_recip(prev[0])
                    elif b == 2:
                        epi_hilo()
                    elif b == 3:
                        epi_partB(*prev)
                        prev = None
            kind, val = p_ready[nb2 - 1]
            PE.wait_ge(sDV.h if kind == "dv" else sAC.h, val)
            bbl = (bidx + nb2 - 1) % 2
            for j in range(2):
                kt_i = 2 * (nb2 - 1) + j
                mm = PE.matmul(pv[hp],
                               vp[:, kt_i * 272 + g * 68: kt_i * 272 + g * 68 + 65],
                               pt[bbl][:, j * 512:(j + 1) * 512],
                               start=False, stop=(j == 1))
            sPE.inc(mm)
            pt_free[bbl] = sPE.n
            prev = (hp, oc, h)
            epi_state["pv_done"] = sPE.n
            bidx += nb2
            it += 1
        if h == 7:
            PL.wait_ge(sPE.h, sPE.n)
            sWT.inc(PL.dma_start(wos[:, :], wo.rearrange("(n p) m -> p n m", p=128)), 16)
            wos_ld = sWT.n
            s1_ld = load_s1(0)
    if prev is not None:
        epi_recip(prev[0])
        epi_hilo()
        epi_partB(*prev)
        prev = None


    pco = [PB1, PB2]
    ph = [PB3[:, 0:256], PB4[:, 0:256]]
    pa2 = [PB5[0:1, 0:512], PB6[0:1, 0:512]]
    rcp = nc.alloc_sbuf_tensor_at("rcp", [1, D], F32, offset=M["_RA"] + 20480)
    b2pf = nc.alloc_sbuf_tensor_at("b2pf", [1, D], F32, offset=M["_RA"] + 24576)
    abufB = nc.alloc_sbuf_tensor_at("abufB", [128, 1024], BF16, offset=M["_RA"] + 12288)
    ab2 = [abuf, abufB]

    b_done_pe, b_done_dv, b_done_st = sPE.n, sDV.n, sST.n
    PL.wait_ge(sPE.h, b_done_pe)
    s2h1_ld = load_s2(1)

    ACT.wait_ge(sWT.h, s1_ld)
    ACT.wait_ge(sST.h, b_done_st)
    for ch in range(4):
        sAC.inc(ACT.activation(s1t[:, ch * 4096:(ch + 1) * 4096],
                               s1t[:, ch * 4096:(ch + 1) * 4096], AF.Sign))
    s1sign = {0: sAC.n}

    SP.wait_ge(sST.h, b_done_st)
    att_pe = {}
    a2mm_pe = {}
    oac_free = [sC2.n, sC2.n]
    pco_free = [sDV.n, sDV.n]
    for tt in range(16):
        bi = tt % 2
        ai = tt % 4
        if tt >= 4:
            SP.wait_ge(sPE.h, att_pe[tt - 4])
        sC1.inc(SP.dma_start(
            att[ai][:, :],
            AP(attn_hbm, tt * 128, [[TOWN, 128], [128 * TOWN, 8], [1, 128]])), 16)
        PE.wait_ge(sC1.h, sC1.n)
        if tt == 0:
            PE.wait_ge(sWT.h, wos_ld)
        PE.wait_ge(sDV.h, pco_free[bi])
        for nc_ in range(2):
            PE.matmul(pco[bi][:, nc_ * 512:(nc_ + 1) * 512], ones_l[0:1, 0:128],
                      bobs[0:1, nc_ * 512:(nc_ + 1) * 512], start=True, stop=False)
            for dd in range(8):
                mm = PE.matmul(pco[bi][:, nc_ * 512:(nc_ + 1) * 512],
                               att[ai][:, dd * 128:(dd + 1) * 128],
                               wos[:, dd * 1024 + nc_ * 512: dd * 1024 + nc_ * 512 + 512],
                               start=False, stop=(dd == 7))
        sPE.inc(mm)
        att_pe[tt] = sPE.n
        i = tt
        ACT.wait_ge(sWT.h, s2h1_ld - 16 * (15 - i))
        if i >= 2:
            ACT.wait_ge(sPE.h, a2mm_pe[i - 2])
        sAC.inc(ACT.activation(ab2[i % 2][:, :], s2t[:, i * 1024:(i + 1) * 1024], AF.Abs))
        PE.wait_ge(sAC.h, sAC.n)
        for nc_ in range(2):
            mm = PE.matmul(pa2[nc_], ones_l[:, 0:1],
                           ab2[i % 2][:, nc_ * 512:(nc_ + 1) * 512],
                           start=(i == 0), stop=False)
        sPE.inc(mm)
        a2mm_pe[i] = sPE.n
        DVE.wait_ge(sPE.h, att_pe[tt])
        DVE.wait_ge(sC2.h, oac_free[bi])
        DVE.tensor_copy(oac[bi][:, :], pco[bi][:, :])
        sDV.inc(DVE.engine_nop())
        pco_free[bi] = sDV.n
        SP.wait_ge(sDV.h, sDV.n)
        sC2.inc(SP.dma_start(oacc_t[tt], oac[bi][:, :]), 16)
        oac_free[bi] = sC2.n

    abs1_ac = sAC.n
    PL.wait_ge(sAC.h, abs1_ac)
    s2h0_ld = load_s2(0)
    a2mm0 = {}
    for i in range(16):
        ACT.wait_ge(sWT.h, s2h0_ld - 16 * (15 - i))
        if i >= 2:
            ACT.wait_ge(sPE.h, a2mm0[i - 2])
        sAC.inc(ACT.activation(ab2[i % 2][:, :], s2t[:, i * 1024:(i + 1) * 1024], AF.Abs))
        PE.wait_ge(sAC.h, sAC.n)
        for nc_ in range(2):
            mm = PE.matmul(pa2[nc_], ones_l[:, 0:1],
                           ab2[i % 2][:, nc_ * 512:(nc_ + 1) * 512],
                           start=False, stop=(i == 15))
        sPE.inc(mm)
        a2mm0[i] = sPE.n
    abs_pe = sPE.n
    for ch in range(4):
        sAC.inc(ACT.activation(s2t[:, ch * 4096:(ch + 1) * 4096],
                               s2t[:, ch * 4096:(ch + 1) * 4096], AF.Sign))
    s2sign = {0: sAC.n}

    DVE.wait_ge(sPE.h, abs_pe)
    DVE.tensor_copy(a2row[0:1, 0:512], pa2[0])
    DVE.tensor_copy(a2row[0:1, 512:1024], pa2[1])
    DVE.engine_nop(); DVE.engine_nop()
    DVE.tensor_scalar(a2row[0:1, :], a2row[0:1, :], 1.0 / 4096.0, None, ALU.mult)
    DVE.engine_nop(); DVE.engine_nop()
    DVE.reciprocal(rcp[0:1, :], a2row[0:1, :])
    DVE.engine_nop(); DVE.engine_nop()
    DVE.tensor_tensor(b2pf[0:1, :], b2rs[0:1, :], rcp[0:1, :], ALU.mult)
    DVE.engine_nop(); DVE.engine_nop()
    DVE.tensor_copy(b2pb[0:1, :], b2pf[0:1, :])
    DVE.tensor_copy(a2hl[0:1, 0:1024], a2row[0:1, :])
    DVE.engine_nop(); DVE.engine_nop()
    DVE.tensor_tensor(a2hl[0:1, 1024:2048], a2row[0:1, :], a2hl[0:1, 0:1024], ALU.subtract)
    sDV.inc(DVE.engine_nop())
    PE.wait_ge(sDV.h, sDV.n)
    PE.wait_ge(sDV.h, pco_free[0])
    for nc_ in range(2):
        PE.matmul(pco[0][:, nc_ * 512:(nc_ + 1) * 512], ones_l[0:1, 0:128],
                  a2hl[0:1, nc_ * 512:(nc_ + 1) * 512], start=True, stop=False)
        mm = PE.matmul(pco[0][:, nc_ * 512:(nc_ + 1) * 512], ones_l[0:1, 0:128],
                       a2hl[0:1, 1024 + nc_ * 512: 1024 + (nc_ + 1) * 512],
                       start=False, stop=True)
    sPE.inc(mm)
    DVE.wait_ge(sPE.h, sPE.n)
    DVE.tensor_copy(a2bc[:, :], pco[0][:, :])
    sDV.inc(DVE.engine_nop())
    pco_free[0] = sDV.n
    a2_done = sDV.n

    m1_end = {}

    def emit_ffn_half(jh):
        gelu_done = sAC.n
        for cc in range(8):
            for jt in range(16):
                bk = jt % 2
                if cc == 0 and jt == 0:
                    PE.wait_ge(sAC.h, s1sign[jh])
                    PE.wait_ge(sDV.h, b_done_dv)
                    ACT.wait_ge(sDV.h, alpha1_done)
                PE.wait_ge(sAC.h, gelu_done if jt < 2 else sAC.n - 1)
                for dt in range(8):
                    mm = PE.matmul(ph[bk],
                                   s1t[:, dt * 2048 + jt * 128: dt * 2048 + jt * 128 + 128],
                                   xnT_own[:, dt * TOWN + cc * 256: dt * TOWN + cc * 256 + 256],
                                   start=(dt == 0), stop=(dt == 7))
                sPE.inc(mm)
                ACT.wait_ge(sPE.h, sPE.n)
                sAC.inc(ACT.activation(hbuf[:, jt * 256:(jt + 1) * 256], ph[bk],
                                       AF.Gelu, bias=b12s[:, jh * 16 + jt: jh * 16 + jt + 1],
                                       scale=alpha1s[:, jh * 16 + jt: jh * 16 + jt + 1]))
            if cc == 7:
                m1_end[jh] = sPE.n
            gelu_done = sAC.n
            for tl in range(2):
                tt = cc * 2 + tl
                bi = tt % 2
                ACT.wait_ge(sDV.h, pco_free[bi])
                sC3.inc(ACT.dma_start(oac[bi][:, :], oacc_t[tt]), 16)
                if jh == 1:
                    sC3.inc(ACT.dma_start(xre[bi][:, :], x_own_t[tt]), 16)
                PE.wait_ge(sAC.h, gelu_done)
                PE.wait_ge(sDV.h, pco_free[bi])
                if cc == 0 and tl == 0:
                    PE.wait_ge(sAC.h, s2sign[jh])
                    if jh == 0:
                        PE.wait_ge(sDV.h, a2_done)
                for nc_ in range(2):
                    if jh == 0:
                        PE.matmul(pco[bi][:, nc_ * 512:(nc_ + 1) * 512], ones_l[0:1, 0:128],
                                  b2pb[0:1, nc_ * 512:(nc_ + 1) * 512], start=True, stop=False)
                    for jt in range(16):
                        mm = PE.matmul(pco[bi][:, nc_ * 512:(nc_ + 1) * 512],
                                       hbuf[:, jt * 256 + tl * 128: jt * 256 + tl * 128 + 128],
                                       s2t[:, jt * 1024 + nc_ * 512: jt * 1024 + nc_ * 512 + 512],
                                       start=(jh == 1 and jt == 0), stop=(jt == 15))
                sPE.inc(mm)
                DVE.wait_ge(sPE.h, sPE.n)
                DVE.wait_ge(sC3.h, sC3.n)
                DVE.wait_ge((sC2 if jh == 0 else sST).h, ost_free[bi])
                DVE.tensor_tensor(ost[bi][:, :], pco[bi][:, :], a2bc[:, :], ALU.mult)
                DVE.engine_nop(); DVE.engine_nop()
                DVE.tensor_tensor(ost[bi][:, :], ost[bi][:, :], oac[bi][:, :], ALU.add)
                if jh == 1:
                    DVE.engine_nop(); DVE.engine_nop()
                    DVE.tensor_tensor(ost[bi][:, :], ost[bi][:, :], xre[bi][:, :], ALU.add)
                sDV.inc(DVE.engine_nop())
                pco_free[bi] = sDV.n
                SP.wait_ge(sDV.h, sDV.n)
                if jh == 0:
                    sC2.inc(SP.dma_start(oacc_t[tt], ost[bi][:, :]), 16)
                    ost_free[bi] = sC2.n
                else:
                    sST.inc(SP.dma_start(y_t[tt], ost[bi][:, :]), 16)
                    ost_free[bi] = sST.n

    ost_free = [sC2.n, sC2.n]
    ACT.wait_ge(sC2.h, sC2.n)
    emit_ffn_half(0)
    PL.wait_ge(sPE.h, m1_end[0])
    s1_ld1 = load_s1(1)
    h0_pe = sPE.n
    PL.wait_ge(sPE.h, h0_pe)
    s2_ld1 = load_s2(1)
    ACT.wait_ge(sWT.h, s1_ld1)
    for ch in range(4):
        sAC.inc(ACT.activation(s1t[:, ch * 4096:(ch + 1) * 4096],
                               s1t[:, ch * 4096:(ch + 1) * 4096], AF.Sign))
    s1sign[1] = sAC.n
    ACT.wait_ge(sWT.h, s2_ld1)
    for ch in range(4):
        sAC.inc(ACT.activation(s2t[:, ch * 4096:(ch + 1) * 4096],
                               s2t[:, ch * 4096:(ch + 1) * 4096], AF.Sign))
    s2sign[1] = sAC.n
    ACT.wait_ge(sC2.h, sC2.n)
    emit_ffn_half(1)

    PL.wait_ge(sST.h, sST.n)
    return nc


def _rope_tables(cols):
    r = np.arange(128)
    i = (r % 64) // 2
    freq = np.exp(-(2.0 * i / 64.0) * np.log(np.float64(10000.0)))
    ang = freq[:, None] * cols[None, :].astype(np.float64)
    c = np.cos(ang)
    s = np.sin(ang) * np.where(r % 2 == 0, -1.0, 1.0)[:, None]
    out = []
    for v in (c, s):
        hi = v.astype(np.float32)
        lo = (v - hi.astype(np.float64)).astype(np.float32)
        out += [hi, lo]
    return out


def _masks(half):
    tk = np.arange(128)[:, None]
    tq = np.arange(512)[None, :]
    tri = np.zeros((128, 2048), dtype=np.float32)
    for j in range(4):
        tri[:, j * 512:(j + 1) * 512] = (tq >= tk + 128 * j).astype(np.float32)
    ones = np.ones((128, 2048), dtype=np.float32)
    zeros = np.zeros((128, 2048), dtype=np.float32)
    m = np.zeros((128, 4 * 2048), dtype=np.float32)
    for cp in range(2):
        a, bb = (tri, zeros) if (cp + half) % 2 == 0 else (ones, tri)
        m[:, (cp * 2) * 2048:(cp * 2 + 1) * 2048] = a
        m[:, (cp * 2 + 1) * 2048:(cp * 2 + 2) * 2048] = bb
    return m


def make_core_inputs(core, I):
    b, half = core // 2, core % 2
    x = np.asarray(I["L0_attn_x"][b], dtype=np.float32)
    rs = OWN_RANGES[half]
    x_own = np.concatenate([x[a:bb] for a, bb in rs], axis=0)
    own_cols = np.concatenate([np.arange(a, bb) for a, bb in rs])
    ch, cl, sh, sl = _rope_tables(np.arange(T))
    cqh_, cql_, sqh_, sql_ = _rope_tables(own_cols)
    f32 = lambda a: np.ascontiguousarray(a, dtype=np.float32)
    return {
        "x_all": f32(x),
        "x_own": f32(x_own),
        "wq": f32(I["L0_attn_Wq"]), "wk": f32(I["L0_attn_Wk"]),
        "wv": f32(I["L0_attn_Wv"]), "wo": f32(I["L0_attn_Wo"]),
        "w1t": f32(np.asarray(I["L0_ffn_w1"]).T),
        "w2t": f32(np.asarray(I["L0_ffn_w2"]).T),
        "w1n": f32(I["L0_ffn_w1"]),
        "bq2": f32(np.asarray(I["L0_attn_bq"]).reshape(8, 128).T),
        "bk2": f32(np.asarray(I["L0_attn_bk"]).reshape(2, 128).T),
        "b12": f32(np.asarray(I["L0_ffn_b1"]).reshape(32, 128).T),
        "bvr": f32(np.asarray(I["L0_attn_bv"]).reshape(1, 256)),
        "bor": f32(np.asarray(I["L0_attn_bo"]).reshape(1, D)),
        "b2r": f32(np.asarray(I["L0_ffn_b2"]).reshape(1, D)),
        "ckh": f32(ch), "ckl": f32(cl), "skh": f32(sh), "skl": f32(sl),
        "cqh": f32(cqh_), "cql": f32(cql_), "sqh": f32(sqh_), "sql": f32(sql_),
        "mask4": _masks(half),
    }


_NC_CACHE = {}


def kernel(**inputs):
    key = (_STAGE, _DEBUG)
    if key not in _NC_CACHE:
        _NC_CACHE[key] = build_program(stage=_STAGE, debug=_DEBUG)
    nc = _NC_CACHE[key]
    in_maps = [make_core_inputs(c, inputs) for c in range(8)]
    res = run_bass_kernel_spmd(nc, in_maps, core_ids=list(range(8)))
    out = np.zeros((4, T, D), dtype=np.float32)
    for c in range(8):
        b, half = c // 2, c % 2
        yv = np.asarray(res.results[c]["y"])
        o = 0
        for a, bb in OWN_RANGES[half]:
            out[b, a:bb] = yv[o:o + (bb - a)]
            o += bb - a
    return out


# version 12 (final)
# speedup vs baseline: 1.2663x; 1.0086x over previous
import numpy as np
import concourse.bass as bass
import concourse.mybir as mybir
from concourse.bass import AP
from concourse.bass_utils import run_bass_kernel_spmd

F32 = mybir.dt.float32
BF16 = mybir.dt.bfloat16
AF = mybir.ActivationFunctionType
ALU = mybir.AluOpType
AXX = mybir.AxisListType.X

D = 1024
T = 4096
H = 16
G = 4
HID = 4096
TOWN = 2048
KP = 96
EPS = 1e-5
CLAMP = float(2.0 ** 120)
SWAP32 = [i ^ 1 for i in range(32)]

OWN_CHUNKS = {0: [0, 3, 4, 7], 1: [1, 2, 5, 6]}
OWN_RANGES = {h: [(c * 512, (c + 1) * 512) for c in OWN_CHUNKS[h]] for h in (0, 1)}
QCHUNKS = {h: [c * 512 for c in OWN_CHUNKS[h]] for h in (0, 1)}
NBRUN = [2, 4, 6, 8]

_STAGE = 6
_DEBUG = False
_ATT_CH = 4
_ATT_H = 16
_ATT_FORCE_HALF = None


SBUF_BASE = 17024


def _mmap():
    m = {}
    cur = [SBUF_BASE]

    def add(name, nbytes, at=None):
        if at is None:
            at = cur[0]
            cur[0] += (nbytes + 31) & ~31
        m[name] = at
        return at

    add("xnT_own", 8 * TOWN * 2)
    add("vp", 32 * 272 * 2)
    add("qt", 16 * TOWN * 2)
    add("kt", 4 * T * 2)
    add("maskt", 4 * 512 * 2)
    add("bq2s", 8 * 4); add("bk2s", 2 * 4); add("b12s", 32 * 4)
    add("bvbs", 256 * 2); add("bobs", D * 2)
    add("b2rs", D * 4)
    add("alpha1s", 32 * 4)
    add("ones_l", 128 * 2)
    add("cb", 16 * 4)
    add("cbt", 64 * 4)
    add("cqs", TOWN * 2); add("sqs", TOWN * 2)
    ra = cur[0]
    m["_RA"] = ra
    add("xbuf0", 0, ra); add("xbuf1", 0, ra + 16384)
    add("xtok0", 0, m["qt"] + 16384); add("xtok1", 0, m["qt"] + 20480)
    add("lnst", 0, ra + 32768)
    add("xnT_q", 0, ra)
    add("kstq", 0, ra + 24576)
    add("shbuf", 0, ra + 28672)
    add("wqs", 0, ra)
    add("qstc", 0, ra + 16384)
    add("q2st", 0, ra + 24576)
    add("shq", 0, ra + 28672)
    add("pt0", 0, ra); add("pt1", 0, ra + 4096)
    add("astg", 0, ra + 8192)
    add("srow", 0, ra + 9216)
    add("rrow", 0, ra + 11264)
    add("rhl", 0, ra + 13312)
    add("mskb", 0, ra + 16384)
    q0 = m["qt"]
    add("cks", 0, q0); add("sks", 0, q0 + 8192)
    add("hilo0", 0, q0 + 16384); add("hilo1", 0, q0 + 32768)
    add("wkv", 0, q0 + 49152)
    add("s1t", 0, q0)
    add("s2t", 0, q0 + 32768)
    k0 = m["kt"]
    add("wos", 0, k0)
    add("hbuf", 0, k0 + 16384)
    add("xre0", 0, m["mskb"]); add("xre1", 0, m["mskb"] + 4096)
    v0 = m["vp"]
    add("oac0", 0, v0); add("oac1", 0, v0 + 4096)
    add("ost0", 0, v0 + 8192); add("ost1", 0, v0 + 12288)
    add("att0", 0, ra)
    add("att1", 0, ra + 2048)
    add("w1n0", 0, ra + 4096); add("w1n1", 0, ra + 6144)
    add("abuf", 0, ra + 8192)
    add("b2pb", 0, ra + 10240)
    add("a2hl", 0, ra + 12288)
    add("a2row", 0, ra + 16384)
    add("a2bc", 0, m["maskt"])
    m["_END"] = ra + 31744
    assert m["_END"] <= 229344
    assert m["_END"] <= 229344
    return m


class Sem:
    def __init__(self, nc, name):
        self.h = nc.alloc_semaphore(name=name)
        self.n = 0

    def inc(self, instr, k=1):
        self.n += k
        return instr.then_inc(self.h, k)


def build_program(stage=5, debug=False):
    nc = bass.Bass("TRN2", target_bir_lowering=False, detect_race_conditions=False)
    M = _mmap()

    def din(name, shape, dtype=F32):
        return nc.dram_tensor(name, list(shape), dtype, kind="ExternalInput")

    x_all = din("x_all", [T, D])
    x_own = din("x_own", [TOWN, D])
    wq = din("wq", [D, D]); wk = din("wk", [D, 256]); wv = din("wv", [D, 256])
    wo = din("wo", [D, D])
    w1t = din("w1t", [D, HID]); w2t = din("w2t", [HID, D]); w1n = din("w1n", [HID, D])
    bq2 = din("bq2", [128, 8]); bk2 = din("bk2", [128, 2]); b12 = din("b12", [128, 32])
    bvr = din("bvr", [1, 256]); bor = din("bor", [1, D]); b2r = din("b2r", [1, D])
    ckh = din("ckh", [128, T]); ckl = din("ckl", [128, T])
    skh = din("skh", [128, T]); skl = din("skl", [128, T])
    cqh = din("cqh", [128, TOWN]); cql = din("cql", [128, TOWN])
    sqh = din("sqh", [128, TOWN]); sql = din("sql", [128, TOWN])
    mask4 = din("mask4", [128, 4 * 2048])

    y = nc.dram_tensor("y", [TOWN, D], F32, kind="ExternalOutput")
    xn_hbm = nc.dram_tensor("xn_hbm", [T, D], BF16)
    xnown_hbm = nc.dram_tensor("xnown_hbm", [TOWN, D], BF16)
    attn_hbm = nc.dram_tensor("attn_hbm", [D, TOWN], BF16)
    oacc_hbm = nc.dram_tensor("oacc_hbm", [TOWN, D], F32)

    dbg = {}
    if debug:
        dbg["d1"] = nc.dram_tensor("dbg1", [128, 16 * TOWN], BF16, kind="ExternalOutput")
        dbg["d2"] = nc.dram_tensor("dbg2", [128, 64], F32, kind="ExternalOutput")

    def sbt(name, shape, dtype):
        return nc.alloc_sbuf_tensor_at(name, list(shape), dtype, offset=M[name])

    xnT_own = sbt("xnT_own", [128, 8 * TOWN], BF16)
    vp = sbt("vp", [128, 32 * 272], BF16)
    qt = [nc.alloc_sbuf_tensor_at(f"qt{h}", [KP, TOWN], BF16, offset=M["qt"] + h * TOWN * 2) for h in range(H)]
    kt = [nc.alloc_sbuf_tensor_at(f"kt{g}", [KP, T], BF16, offset=M["kt"] + g * T * 2) for g in range(G)]
    mskb = sbt("mskb", [128, 4 * 2048], BF16)
    bq2s = sbt("bq2s", [128, 8], F32); bk2s = sbt("bk2s", [128, 2], F32)
    b12s = sbt("b12s", [128, 32], F32)
    bvbs = sbt("bvbs", [1, 256], BF16); bobs = sbt("bobs", [1, D], BF16)
    b2rs = sbt("b2rs", [1, D], F32)
    alpha1s = sbt("alpha1s", [128, 32], F32)
    ones_l = sbt("ones_l", [128, 128], BF16)
    cbS = sbt("cb", [128, 16], F32)
    cbt = sbt("cbt", [1, 64], F32)
    cqs = sbt("cqs", [128, TOWN], BF16); sqs = sbt("sqs", [128, TOWN], BF16)
    xbuf = [sbt("xbuf0", [128, 4096], F32), sbt("xbuf1", [128, 4096], F32)]
    xtok = [sbt("xtok0", [128, 2048], BF16), sbt("xtok1", [128, 2048], BF16)]
    lnst = sbt("lnst", [128, 128], F32)
    xnT_q = sbt("xnT_q", [128, 8 * 1024], BF16)
    wkvS = sbt("wkv", [128, 4 * 1024], BF16)
    kstq = sbt("kstq", [128, 2 * 1024], BF16)
    shbuf = sbt("shbuf", [128, 1024], BF16)
    wqs = sbt("wqs", [128, 8 * 1024], BF16)
    qstc = sbt("qstc", [128, 8 * 512], BF16)
    q2st = sbt("q2st", [KP, TOWN], BF16)
    shq = sbt("shq", [128, 512], BF16)
    pt = [sbt("pt0", [128, 2048], BF16), sbt("pt1", [128, 2048], BF16)]
    astg = sbt("astg", [128, 512], BF16)
    srow = sbt("srow", [128, 512], F32)
    rrow = sbt("rrow", [128, 512], F32)
    rhl = sbt("rhl", [128, 1024], BF16)
    cks = sbt("cks", [128, T], BF16); sks = sbt("sks", [128, T], BF16)
    hilo = [sbt("hilo0", [128, T], F32), sbt("hilo1", [128, T], F32)]
    s1t = sbt("s1t", [128, 8 * 2048], BF16)
    s2t = sbt("s2t", [128, 16 * 1024], BF16)
    wos = sbt("wos", [128, 8 * 1024], BF16)
    hbuf = sbt("hbuf", [128, 16 * 512], BF16)
    xre = [sbt("xre0", [128, 1024], F32), sbt("xre1", [128, 1024], F32)]
    oac = [sbt("oac0", [128, 1024], F32), sbt("oac1", [128, 1024], F32)]
    ost = [sbt("ost0", [128, 1024], F32), sbt("ost1", [128, 1024], F32)]
    att = [sbt("att0", [128, 8 * 128], BF16), sbt("att1", [128, 8 * 128], BF16),
           nc.alloc_sbuf_tensor_at("att2", [128, 8 * 128], BF16, offset=M["w1n0"]),
           nc.alloc_sbuf_tensor_at("att3", [128, 8 * 128], BF16, offset=M["w1n1"])]
    w1nb = [sbt("w1n0", [128, 1024], BF16), sbt("w1n1", [128, 1024], BF16)]
    abuf = sbt("abuf", [128, 1024], BF16)
    b2pb = sbt("b2pb", [1, D], BF16)
    a2hl = sbt("a2hl", [1, 2048], BF16)
    a2row = sbt("a2row", [1, D], F32)
    a2bc = sbt("a2bc", [128, D], F32)

    SP, PE, ACT, DVE, PL = nc.sync, nc.tensor, nc.scalar, nc.vector, nc.gpsimd

    sLD = Sem(nc, "sLD"); sWT = Sem(nc, "sWT"); sXN = Sem(nc, "sXN")
    sTR = Sem(nc, "sTR"); sDV = Sem(nc, "sDV"); sAC = Sem(nc, "sAC")
    sPE = Sem(nc, "sPE"); sSH = Sem(nc, "sSH"); sST = Sem(nc, "sST")
    sC1 = Sem(nc, "sC1"); sC2 = Sem(nc, "sC2"); sC3 = Sem(nc, "sC3")

    x_all_t = x_all.rearrange("(n p) m -> n p m", p=128)
    x_own_t = x_own.rearrange("(n p) m -> n p m", p=128)
    xn_hbm_t = xn_hbm.rearrange("(n p) m -> n p m", p=128)
    xnown_hbm_t = xnown_hbm.rearrange("(n p) m -> n p m", p=128)
    oacc_t = oacc_hbm.rearrange("(n p) m -> n p m", p=128)
    y_t = y.rearrange("(n p) m -> n p m", p=128)

    for dst, src in ((bq2s, bq2), (bk2s, bk2), (b12s, b12), (b2rs, b2r)):
        sLD.inc(SP.dma_start(dst[:, :], src[:, :]), 16)
    sWT.inc(PL.dma_start(bvbs[:, :], bvr[:, :]), 16)
    sWT.inc(PL.dma_start(bobs[:, :], bor[:, :]), 16)

    for ti, (hsrc, lsrc, dstb) in enumerate(((ckh, ckl, cks), (skh, skl, sks),
                                             (cqh, cql, cqs), (sqh, sql, sqs))):
        n = hsrc.shape[1]
        if ti > 0:
            SP.wait_ge(sDV.h, sDV.n)
        sLD.inc(SP.dma_start(hilo[0][:, 0:n], hsrc[:, :]), 16)
        sLD.inc(SP.dma_start(hilo[1][:, 0:n], lsrc[:, :]), 16)
        DVE.wait_ge(sLD.h, sLD.n)
        DVE.tensor_tensor(hilo[0][:, 0:n], hilo[0][:, 0:n], hilo[1][:, 0:n], ALU.add)
        sDV.inc(DVE.tensor_copy(dstb[:, 0:n], hilo[0][:, 0:n]))
    DVE.memset(ones_l[:, :], 1.0)
    sDV.inc(DVE.memset(vp[:, :], 1.0))
    consts_done = sDV.n

    ln_groups = [("all", gi) for gi in range(8)] + [("own", gi) for gi in range(4)]
    grp_dve_done = []
    grp_st_done = []
    N2 = lambda: (DVE.engine_nop(), DVE.engine_nop())
    for idx, (kind, gi) in enumerate(ln_groups):
        bi = idx % 2
        src_t = x_all_t if kind == "all" else x_own_t
        dst_t = xn_hbm_t if kind == "all" else xnown_hbm_t
        if idx >= 2:
            SP.wait_ge(sDV.h, grp_dve_done[idx - 2])
        for tl in range(4):
            sLD.inc(SP.dma_start(xbuf[bi][:, tl * 1024:(tl + 1) * 1024], src_t[4 * gi + tl]), 16)
        DVE.wait_ge(sLD.h, sLD.n)
        for tl in range(4):
            DVE.bn_stats(lnst[:, 12 * tl:12 * tl + 6], xbuf[bi][:, tl * 1024:tl * 1024 + 512])
            DVE.bn_stats(lnst[:, 12 * tl + 6:12 * tl + 12], xbuf[bi][:, tl * 1024 + 512:tl * 1024 + 1024])
            DVE.bn_aggr(lnst[:, 48 + 4 * tl:50 + 4 * tl], lnst[:, 12 * tl:12 * tl + 6])
            DVE.bn_aggr(lnst[:, 50 + 4 * tl:52 + 4 * tl], lnst[:, 12 * tl + 6:12 * tl + 12])
        m1 = AP(lnst, 48, [[128, 128], [4, 4]]); v1 = AP(lnst, 49, [[128, 128], [4, 4]])
        m2 = AP(lnst, 50, [[128, 128], [4, 4]]); v2 = AP(lnst, 51, [[128, 128], [4, 4]])
        N2()
        DVE.tensor_tensor(lnst[:, 64:68], m1, m2, ALU.subtract)
        DVE.tensor_tensor(lnst[:, 68:72], m1, m2, ALU.add)
        DVE.tensor_tensor(lnst[:, 76:80], v1, v2, ALU.add)
        DVE.tensor_tensor(lnst[:, 72:76], lnst[:, 64:68], lnst[:, 64:68], ALU.mult)
        DVE.tensor_scalar(lnst[:, 68:72], lnst[:, 68:72], 0.5, None, ALU.mult)
        DVE.tensor_scalar(lnst[:, 76:80], lnst[:, 76:80], 0.5, None, ALU.mult)
        DVE.tensor_scalar(lnst[:, 72:76], lnst[:, 72:76], 0.25, None, ALU.mult)
        N2()
        DVE.tensor_tensor(lnst[:, 80:84], lnst[:, 76:80], lnst[:, 72:76], ALU.add)
        N2()
        DVE.tensor_scalar(lnst[:, 84:88], lnst[:, 80:84], EPS, None, ALU.add)
        sDV.inc(DVE.engine_nop())
        ACT.wait_ge(sDV.h, sDV.n)
        sAC.inc(ACT.activation(lnst[:, 88:92], lnst[:, 84:88], AF.Sqrt))
        DVE.wait_ge(sAC.h, sAC.n)
        DVE.reciprocal(lnst[:, 92:96], lnst[:, 88:92])
        N2()
        DVE.tensor_tensor(lnst[:, 96:100], lnst[:, 88:92], lnst[:, 92:96], ALU.mult)
        N2()
        DVE.tensor_scalar(lnst[:, 96:100], lnst[:, 96:100], -1.0, 2.0, ALU.mult, ALU.add)
        N2()
        DVE.tensor_tensor(lnst[:, 100:104], lnst[:, 92:96], lnst[:, 96:100], ALU.mult)
        N2()
        DVE.tensor_tensor(lnst[:, 104:108], lnst[:, 68:72], lnst[:, 100:104], ALU.mult)
        N2()
        DVE.tensor_scalar(lnst[:, 104:108], lnst[:, 104:108], -1.0, None, ALU.mult)
        N2()
        for half2 in range(2):
            if idx >= 1:
                DVE.wait_ge(sXN.h, grp_st_done[-1][half2] if idx >= 1 else 0)
            for tl2 in range(2):
                tl = half2 * 2 + tl2
                DVE.tensor_scalar(xtok[half2][:, tl2 * 1024:(tl2 + 1) * 1024],
                                  xbuf[bi][:, tl * 1024:(tl + 1) * 1024],
                                  lnst[:, 100 + tl:101 + tl], lnst[:, 104 + tl:105 + tl],
                                  ALU.mult, ALU.add)
            sDV.inc(DVE.engine_nop())
            SP.wait_ge(sDV.h, sDV.n)
            sXN.inc(SP.dma_start(dst_t[4 * gi + 2 * half2], xtok[half2][:, 0:1024]), 16)
            sXN.inc(SP.dma_start(dst_t[4 * gi + 2 * half2 + 1], xtok[half2][:, 1024:2048]), 16)
            if half2 == 0:
                st_h0 = sXN.n
        grp_st_done.append((st_h0, sXN.n))
        grp_dve_done.append(sDV.n)

    SP.wait_ge(sXN.h, sXN.n)
    for ds in range(8):
        sTR.inc(SP.dma_start(xnT_own[:, ds * TOWN:(ds + 1) * TOWN],
                             xnown_hbm[:, ds * 128:(ds + 1) * 128], transpose=True), 16)
    trow_done = sTR.n
    sWT.inc(PL.dma_start(wkvS[:, 0:2048], wk.rearrange("(n p) m -> p n m", p=128)), 16)
    sWT.inc(PL.dma_start(wkvS[:, 2048:4096], wv.rearrange("(n p) m -> p n m", p=128)), 16)
    wkv_ld = sWT.n

    a1st = [nc.alloc_sbuf_tensor_at(f"a1st{i}", [128, 1024], BF16,
                                    offset=M["maskt"] + i * 2048) for i in range(2)]
    w1n_t = w1n.rearrange("(n p) m -> n p m", p=128)
    a1red = []
    for gi in range(32):
        bi = gi % 2
        if gi >= 2:
            PL.wait_ge(sDV.h, a1red[gi - 2])
        sWT.inc(PL.dma_start(a1st[bi][:, :], w1n_t[gi]), 16)
        DVE.wait_ge(sWT.h, sWT.n)
        DVE.reduce_sum(alpha1s[:, gi:gi + 1], a1st[bi][:, :],
                       axis=AXX, apply_absolute_value=True)
        sDV.inc(DVE.engine_nop())
        a1red.append(sDV.n)
    DVE.engine_nop(); DVE.engine_nop()
    DVE.tensor_scalar(alpha1s[:, :], alpha1s[:, :], 1.0 / 1024.0, None, ALU.mult)
    DVE.engine_nop(); DVE.engine_nop()
    sDV.inc(DVE.engine_nop())
    alpha1_done = sDV.n

    if debug and stage <= 1:
        SP.wait_ge(sTR.h, trow_done)
        sST.inc(SP.dma_start(dbg["d1"][:, 0:8 * TOWN], xnT_own[:, :]), 16)
        PL.wait_ge(sST.h, sST.n)
        return nc

    import contextlib
    PB1 = nc.alloc_psum_tensor("PB1", [128, 1024], F32)
    PB2 = nc.alloc_psum_tensor("PB2", [128, 1024], F32)
    PB3 = nc.alloc_psum_tensor("PB3", [128, 512], F32)
    PB4 = nc.alloc_psum_tensor("PB4", [128, 512], F32)
    PB5 = nc.alloc_psum_tensor("PB5", [128, 512], F32)
    PB6 = nc.alloc_psum_tensor("PB6", [128, 512], F32)
    pp = [PB1[:, 0:512], PB1[:, 512:1024]]
    pS = PB6[0:1, 0:512]
    pCb = PB6[:, 0:16]

    PE.wait_ge(sWT.h, wkv_ld)
    ACT.wait_ge(sDV.h, consts_done)
    q_done_pe = []
    kv_rope_done = []
    for qi in range(4):
        if qi > 0:
            SP.wait_ge(sPE.h, q_done_pe[qi - 1])
            SP.wait_ge(sSH.h, sSH.n)
        for ds in range(8):
            sTR.inc(SP.dma_start(xnT_q[:, ds * 1024:(ds + 1) * 1024],
                                 xn_hbm[qi * 1024:(qi + 1) * 1024, ds * 128:(ds + 1) * 128],
                                 transpose=True), 16)
        PE.wait_ge(sTR.h, sTR.n)
        if qi > 0:
            ACT.wait_ge(sDV.h, kv_rope_done[qi - 1])
        for mt in range(2):
            for ccq in range(2):
                bank = (mt * 2 + ccq) % 2
                PE.wait_ge(sAC.h, sAC.n)
                for dt in range(8):
                    mm = PE.matmul(pp[bank],
                              wkvS[:, dt * 256 + mt * 128: dt * 256 + mt * 128 + 128],
                              xnT_q[:, dt * 1024 + ccq * 512: dt * 1024 + ccq * 512 + 512],
                              start=(dt == 0), stop=(dt == 7))
                sPE.inc(mm)
                ACT.wait_ge(sPE.h, sPE.n)
                sAC.inc(ACT.activation(kstq[:, mt * 1024 + ccq * 512: mt * 1024 + ccq * 512 + 512],
                                       pp[bank], AF.Identity, bias=bk2s[:, mt:mt + 1]))
        for vt in range(8):
            tile = qi * 8 + vt
            bank = vt % 2
            PE.wait_ge(sAC.h, sAC.n)
            for dt in range(8):
                PE.matmul(pp[bank][:, 0:256],
                          xnT_q[:, dt * 1024 + vt * 128: dt * 1024 + vt * 128 + 128],
                          wkvS[:, 2048 + dt * 256: 2048 + (dt + 1) * 256],
                          start=(dt == 0), stop=False)
            mm = PE.matmul(pp[bank][:, 0:256], ones_l[0:1, 0:128], bvbs[0:1, :],
                      start=False, stop=True)
            sPE.inc(mm)
            ACT.wait_ge(sPE.h, sPE.n)
            sAC.inc(ACT.activation(
                AP(vp, tile * 272, [[32 * 272, 128], [68, 4], [1, 64]]),
                AP(PB1, bank * 512, [[1024, 128], [64, 4], [1, 64]]), AF.Copy))
        q_done_pe.append(sPE.n)
        DVE.wait_ge(sAC.h, sAC.n)
        cA = qi * 1024
        for mt in range(2):
            ks = kstq[:, mt * 1024:(mt + 1) * 1024]
            DVE.stream_shuffle(shbuf[:, :], ks, SWAP32)
            DVE.tensor_tensor(ks, ks, cks[:, cA:cA + 1024], ALU.mult)
            DVE.tensor_tensor(shbuf[:, :], shbuf[:, :], sks[:, cA:cA + 1024], ALU.mult)
            DVE.tensor_tensor(kt[2 * mt][0:64, cA:cA + 1024], ks[0:64, :], shbuf[0:64, :], ALU.add)
            DVE.tensor_tensor(ks[64:128, :], ks[64:128, :], shbuf[64:128, :], ALU.add)
            sDV.inc(DVE.engine_nop())
            SP.wait_ge(sDV.h, sDV.n)
            sSH.inc(SP.dma_start(kt[2 * mt + 1][0:64, cA:cA + 1024], ks[64:128, :]), 16)
        kv_rope_done.append(sDV.n)

    if debug and stage <= 2:
        DVE.wait_ge(sSH.h, sSH.n)
        SP.wait_ge(sDV.h, sDV.n)
        for g in range(G):
            sST.inc(SP.dma_start(dbg["d1"][0:64, g * T:(g + 1) * T], kt[g][0:64, :]), 16)
        sST.inc(SP.dma_start(dbg["d1"][100:101, 0:32 * 272], vp[0:1, :]), 16)
        PL.wait_ge(sST.h, sST.n)
        return nc

    PL.wait_ge(sPE.h, q_done_pe[3])
    sWT.inc(PL.dma_start(wqs[:, :], wq.rearrange("(n p) m -> p n m", p=128)), 16)
    wq_ld = sWT.n
    DVE.wait_ge(sSH.h, sSH.n)
    DVE.wait_ge(sPE.h, q_done_pe[3])
    for h in range(H):
        DVE.memset(qt[h][64:KP, :], 0.0)
    for g in range(G):
        DVE.memset(kt[g][64:KP, :], 0.0)
    sDV.inc(DVE.memset(q2st[64:KP, :], 0.0))
    pads_done = sDV.n

    PE.wait_ge(sWT.h, wq_ld)
    PE.wait_ge(sTR.h, trow_done)
    qrope_done = []
    for cc in range(4):
        for dqt in range(8):
            bank = dqt % 2
            PE.wait_ge(sAC.h, sAC.n)
            if cc > 0:
                PE.wait_ge(sDV.h, qrope_done[cc - 1])
            for dt in range(8):
                mm = PE.matmul(pp[bank],
                          wqs[:, dt * 1024 + dqt * 128: dt * 1024 + dqt * 128 + 128],
                          xnT_own[:, dt * TOWN + cc * 512: dt * TOWN + cc * 512 + 512],
                          start=(dt == 0), stop=(dt == 7))
            sPE.inc(mm)
            ACT.wait_ge(sPE.h, sPE.n)
            sAC.inc(ACT.activation(qstc[:, dqt * 512:(dqt + 1) * 512], pp[bank],
                                   AF.Identity, bias=bq2s[:, dqt:dqt + 1]))
        DVE.wait_ge(sAC.h, sAC.n)
        DVE.wait_ge(sDV.h, pads_done)
        co = cc * 512
        for dqt in range(8):
            qs = qstc[:, dqt * 512:(dqt + 1) * 512]
            DVE.stream_shuffle(shq[:, :], qs, SWAP32)
            DVE.tensor_tensor(qs, qs, cqs[:, co:co + 512], ALU.mult)
            DVE.tensor_tensor(shq[:, :], shq[:, :], sqs[:, co:co + 512], ALU.mult)
            DVE.tensor_tensor(qt[2 * dqt][0:64, co:co + 512], qs[0:64, :], shq[0:64, :], ALU.add)
            DVE.tensor_tensor(qs[64:128, :], qs[64:128, :], shq[64:128, :], ALU.add)
            sDV.inc(DVE.engine_nop())
            SP.wait_ge(sDV.h, sDV.n)
            sSH.inc(SP.dma_start(qt[2 * dqt + 1][0:64, co:co + 512], qs[64:128, :]), 16)
        sDV.inc(DVE.engine_nop())
        qrope_done.append(sDV.n)
    DVE.wait_ge(sSH.h, sSH.n)

    if debug and stage <= 3:
        sDV.inc(DVE.engine_nop())
        SP.wait_ge(sDV.h, sDV.n)
        SP.wait_ge(sSH.h, sSH.n)
        for h in range(H):
            sST.inc(SP.dma_start(dbg["d1"][0:KP, h * TOWN:(h + 1) * TOWN], qt[h][:, :]), 16)
        for g in range(G):
            sST.inc(SP.dma_start(dbg["d1"][KP:KP + 32, g * T:(g + 1) * T], kt[g][32:64, :]), 16)
        PL.wait_ge(sST.h, sST.n)
        return nc

    a_qk_dv = sDV.n

    pS4 = [PB6[0:1, 0:512], PB3[0:1, 0:512], PB4[0:1, 0:512], PB5[0:1, 0:512]]
    N2c = lambda: (DVE.engine_nop(), DVE.engine_nop())

    def max_of_sums(dst_col, init):
        PE.wait_ge(sDV.h, sDV.n)
        for ch in range(4):
            mm = PE.matmul(pS4[ch], ones_l[0:KP, 0:1], q2st[0:KP, ch * 512:(ch + 1) * 512],
                           start=True, stop=True)
        sPE.inc(mm)
        DVE.wait_ge(sPE.h, sPE.n)
        for ch in range(4):
            DVE.reduce_max(cbt[0:1, ch:ch + 1], pS4[ch], axis=AXX)
        N2c()
        DVE.tensor_tensor(cbt[0:1, 4:6], AP(cbt, 0, [[64, 1], [1, 2]]),
                          AP(cbt, 2, [[64, 1], [1, 2]]), ALU.max)
        N2c()
        DVE.tensor_tensor(cbt[0:1, 6:7], cbt[0:1, 4:5], cbt[0:1, 5:6], ALU.max)
        N2c()
        if init:
            DVE.tensor_copy(cbt[0:1, dst_col:dst_col + 1], cbt[0:1, 6:7])
        else:
            DVE.tensor_tensor(cbt[0:1, dst_col:dst_col + 1],
                              cbt[0:1, dst_col:dst_col + 1], cbt[0:1, 6:7], ALU.max)
        N2c()
        sDV.inc(DVE.engine_nop())

    for g in range(G):
        for hh in range(2):
            DVE.wait_ge(sPE.h, sPE.n)
            DVE.tensor_tensor(q2st[0:64, :], kt[g][0:64, hh * 2048:(hh + 1) * 2048],
                              kt[g][0:64, hh * 2048:(hh + 1) * 2048], ALU.mult)
            sDV.inc(DVE.engine_nop())
            max_of_sums(32 + g, hh == 0)
    for h in range(H):
        DVE.wait_ge(sPE.h, sPE.n)
        DVE.tensor_tensor(q2st[0:64, :], qt[h][0:64, :], qt[h][0:64, :], ALU.mult)
        sDV.inc(DVE.engine_nop())
        max_of_sums(36 + h, True)
    ACT.wait_ge(sDV.h, sDV.n)
    sAC.inc(ACT.activation(cbt[0:1, 32:52], cbt[0:1, 32:52], AF.Sqrt))
    DVE.wait_ge(sAC.h, sAC.n)
    DVE.tensor_tensor(cbt[0:1, 36:52], cbt[0:1, 36:52],
                      AP(cbt, 32, [[64, 1], [1, 4], [0, 4]]), ALU.mult)
    DVE.engine_nop(); DVE.engine_nop()
    DVE.tensor_scalar(cbt[0:1, 36:52], cbt[0:1, 36:52], -0.125, None, ALU.mult)
    DVE.engine_nop(); DVE.engine_nop()
    sDV.inc(DVE.tensor_copy(shq[0:1, 0:16], cbt[0:1, 36:52]))
    PE.wait_ge(sDV.h, sDV.n)
    mm = PE.matmul(pCb, ones_l[0:1, 0:128], shq[0:1, 0:16], start=True, stop=True)
    sPE.inc(mm)
    DVE.wait_ge(sPE.h, sPE.n)
    sDV.inc(DVE.tensor_copy(cbS[:, :], pCb))
    a_done_dv = sDV.n

    if debug and stage <= 4:
        SP.wait_ge(sDV.h, sDV.n)
        for h in range(H):
            sST.inc(SP.dma_start(dbg["d1"][0:KP, h * TOWN:(h + 1) * TOWN], qt[h][:, :]), 16)
        for g in range(G):
            sST.inc(SP.dma_start(dbg["d1"][KP:KP + 32, g * T:(g + 1) * T],
                                 kt[g][32:64, :]), 16)
        sST.inc(SP.dma_start(dbg["d2"][:, 0:16], cbS[:, :]), 16)
        sST.inc(SP.dma_start(dbg["d2"][1:2, 0:48], cbt[0:1, 16:64]), 16)
        PL.wait_ge(sST.h, sST.n)
        return nc

    PL.wait_ge(sPE.h, sPE.n)
    sWT.inc(PL.dma_start(mskb[:, :], mask4[:, :]), 16)

    pblk = [PB1, PB2]
    pv = [PB3[0:65, :], PB4[0:65, :], PB6[0:65, :]]
    prb = PB5

    PE.wait_ge(sDV.h, a_qk_dv)
    PE.wait_ge(sSH.h, sSH.n)
    ACT.wait_ge(sDV.h, a_done_dv)
    DVE.wait_ge(sWT.h, sWT.n)
    exp_done = [sAC.n, sAC.n]
    pt_free = [sPE.n, sPE.n]
    pv_free = [sDV.n, sDV.n, sDV.n]
    bidx = 0
    prev = None
    epi_state = {"prb_free": sDV.n, "astg_free": sST.n, "pv_done": 0,
                 "ac": 0, "dv": 0, "last_hilo": 0, "last_prb": 0}

    def epi_recip(hp):
        DVE.wait_ge(sPE.h, epi_state["pv_done"])
        DVE.tensor_copy(srow[64:65, 0:512], pv[hp][64:65, :])
        DVE.engine_nop(); DVE.engine_nop()
        DVE.reciprocal(rrow[64:65, 0:512], srow[64:65, 0:512])
        sDV.inc(DVE.engine_nop())

    def epi_hilo():
        DVE.wait_ge(sPE.h, epi_state["last_prb"])
        DVE.tensor_copy(rhl[64:65, 0:512], rrow[64:65, 0:512])
        DVE.engine_nop(); DVE.engine_nop()
        DVE.tensor_tensor(rhl[64:65, 512:1024], rrow[64:65, 0:512],
                          rhl[64:65, 0:512], ALU.subtract)
        sDV.inc(DVE.engine_nop())
        epi_state["dv"] = sDV.n

    def epi_partB(hp, oc, h):
        PE.wait_ge(sDV.h, epi_state["dv"])
        PE.wait_ge(sDV.h, epi_state["prb_free"])
        PE.matmul(prb[:, :], ones_l[64:65, 0:128], rhl[64:65, 0:512],
                  start=True, stop=False)
        mm = PE.matmul(prb[:, :], ones_l[64:65, 0:128], rhl[64:65, 512:1024],
                       start=False, stop=True)
        sPE.inc(mm)
        epi_state["last_prb"] = sPE.n
        DVE.wait_ge(sPE.h, sPE.n)
        DVE.wait_ge(sST.h, epi_state["astg_free"])
        DVE.tensor_copy(srow[0:64, 0:512], prb[0:64, :])
        DVE.engine_nop(); DVE.engine_nop()
        DVE.tensor_tensor(astg[0:64, :], pv[hp][0:64, :], srow[0:64, 0:512], ALU.mult)
        sDV.inc(DVE.engine_nop())
        pv_free[hp] = sDV.n
        epi_state["prb_free"] = sDV.n
        SP.wait_ge(sDV.h, sDV.n)
        sST.inc(SP.dma_start(attn_hbm[h * 64:(h + 1) * 64, oc:oc + 512],
                             astg[0:64, :]), 16)
        epi_state["astg_free"] = sST.n

    w2t_t = w2t.rearrange("(n p) m -> n p m", p=128)

    def load_s1(jh):
        sWT.inc(PL.dma_start(
            s1t[:, :], AP(w1t, jh * 2048, [[HID, 128], [128 * HID, 8], [1, 2048]])), 16)
        return sWT.n

    def load_s2(jh):
        for i in range(16):
            sWT.inc(PL.dma_start(s2t[:, i * 1024:(i + 1) * 1024], w2t_t[jh * 16 + i]), 16)
        return sWT.n

    it = 0
    for h in range(H):
        g = h // 4
        for ci in range(4):
            oc = ci * 512
            nb2 = 2 * NBRUN[ci]
            hp = it % 3

            def emit_exp(b):
                bb = (bidx + b) % 2
                ACT.wait_ge(sPE.h, s_done[b])
                ACT.wait_ge(sPE.h, pt_free[bb])
                sAC.inc(ACT.activation(pt[bb][:, 0:1024], pblk[bb][:, :], AF.Exp,
                                       bias=cbS[:, h:h + 1], scale=0.125))
                exp_done[bb] = sAC.n
                if b >= nb2 - 4:
                    s4 = (b - (nb2 - 4)) // 2
                    mo = ((ci % 2) * 2 + s4) * 2048 + (b % 2) * 1024
                    DVE.wait_ge(sAC.h, sAC.n)
                    DVE.tensor_tensor(pt[bb][:, 0:1024], pt[bb][:, 0:1024],
                                      mskb[:, mo:mo + 1024], ALU.mult)
                    sDV.inc(DVE.engine_nop())
                    p_ready[b] = ("dv", sDV.n)
                else:
                    p_ready[b] = ("ac", sAC.n)

            s_done = {}
            p_ready = {}
            for b in range(nb2):
                bb = (bidx + b) % 2
                PE.wait_ge(sAC.h, exp_done[bb])
                for j in range(2):
                    kt_i = 2 * b + j
                    mm = PE.matmul(pblk[bb][:, j * 512:(j + 1) * 512],
                                   kt[g][0:KP, kt_i * 128:(kt_i + 1) * 128],
                                   qt[h][0:KP, oc:oc + 512],
                                   start=True, stop=True)
                sPE.inc(mm)
                s_done[b] = sPE.n
                if b >= 1:
                    kind, val = p_ready[b - 1]
                    PE.wait_ge(sDV.h if kind == "dv" else sAC.h, val)
                    if b == 1:
                        PE.wait_ge(sDV.h, pv_free[hp])
                    bbp = (bidx + b - 1) % 2
                    for j in range(2):
                        kt_i = 2 * (b - 1) + j
                        mm = PE.matmul(pv[hp],
                                       vp[:, kt_i * 272 + g * 68: kt_i * 272 + g * 68 + 65],
                                       pt[bbp][:, j * 512:(j + 1) * 512],
                                       start=(b == 1 and j == 0), stop=False)
                    sPE.inc(mm)
                    pt_free[bbp] = sPE.n
                emit_exp(b)
                if prev is not None:
                    if b == 1:
                        epi_recip(prev[0])
                    elif b == 2:
                        epi_hilo()
                    elif b == 3:
                        epi_partB(*prev)
                        prev = None
            kind, val = p_ready[nb2 - 1]
            PE.wait_ge(sDV.h if kind == "dv" else sAC.h, val)
            bbl = (bidx + nb2 - 1) % 2
            for j in range(2):
                kt_i = 2 * (nb2 - 1) + j
                mm = PE.matmul(pv[hp],
                               vp[:, kt_i * 272 + g * 68: kt_i * 272 + g * 68 + 65],
                               pt[bbl][:, j * 512:(j + 1) * 512],
                               start=False, stop=(j == 1))
            sPE.inc(mm)
            pt_free[bbl] = sPE.n
            prev = (hp, oc, h)
            epi_state["pv_done"] = sPE.n
            bidx += nb2
            it += 1
        if h == 7:
            PL.wait_ge(sPE.h, sPE.n)
            sWT.inc(PL.dma_start(wos[:, :], wo.rearrange("(n p) m -> p n m", p=128)), 16)
            wos_ld = sWT.n
            s1_ld = load_s1(0)
    if prev is not None:
        epi_recip(prev[0])
        epi_hilo()
        epi_partB(*prev)
        prev = None


    pco = [PB1, PB2]
    ph = [PB3[:, 0:512], PB4[:, 0:512]]
    pa2 = [PB5[0:1, 0:512], PB6[0:1, 0:512]]
    rcp = nc.alloc_sbuf_tensor_at("rcp", [1, D], F32, offset=M["_RA"] + 20480)
    b2pf = nc.alloc_sbuf_tensor_at("b2pf", [1, D], F32, offset=M["_RA"] + 24576)
    abufB = nc.alloc_sbuf_tensor_at("abufB", [128, 1024], BF16, offset=M["_RA"] + 12288)
    ab2 = [abuf, abufB]

    b_done_pe, b_done_dv, b_done_st = sPE.n, sDV.n, sST.n
    PL.wait_ge(sPE.h, b_done_pe)
    s2h1_ld = load_s2(1)

    ACT.wait_ge(sWT.h, s1_ld)
    ACT.wait_ge(sST.h, b_done_st)
    for ch in range(4):
        sAC.inc(ACT.activation(s1t[:, ch * 4096:(ch + 1) * 4096],
                               s1t[:, ch * 4096:(ch + 1) * 4096], AF.Sign))
    s1sign = {0: sAC.n}

    SP.wait_ge(sST.h, b_done_st)
    att_pe = {}
    a2mm_pe = {}
    oac_free = [sC2.n, sC2.n]
    pco_free = [sDV.n, sDV.n]
    for tt in range(16):
        bi = tt % 2
        ai = tt % 4
        if tt >= 4:
            SP.wait_ge(sPE.h, att_pe[tt - 4])
        sC1.inc(SP.dma_start(
            att[ai][:, :],
            AP(attn_hbm, tt * 128, [[TOWN, 128], [128 * TOWN, 8], [1, 128]])), 16)
        PE.wait_ge(sC1.h, sC1.n)
        if tt == 0:
            PE.wait_ge(sWT.h, wos_ld)
        PE.wait_ge(sDV.h, pco_free[bi])
        for nc_ in range(2):
            PE.matmul(pco[bi][:, nc_ * 512:(nc_ + 1) * 512], ones_l[0:1, 0:128],
                      bobs[0:1, nc_ * 512:(nc_ + 1) * 512], start=True, stop=False)
            for dd in range(8):
                mm = PE.matmul(pco[bi][:, nc_ * 512:(nc_ + 1) * 512],
                               att[ai][:, dd * 128:(dd + 1) * 128],
                               wos[:, dd * 1024 + nc_ * 512: dd * 1024 + nc_ * 512 + 512],
                               start=False, stop=(dd == 7))
        sPE.inc(mm)
        att_pe[tt] = sPE.n
        i = tt
        ACT.wait_ge(sWT.h, s2h1_ld - 16 * (15 - i))
        if i >= 2:
            ACT.wait_ge(sPE.h, a2mm_pe[i - 2])
        sAC.inc(ACT.activation(ab2[i % 2][:, :], s2t[:, i * 1024:(i + 1) * 1024], AF.Abs))
        PE.wait_ge(sAC.h, sAC.n)
        for nc_ in range(2):
            mm = PE.matmul(pa2[nc_], ones_l[:, 0:1],
                           ab2[i % 2][:, nc_ * 512:(nc_ + 1) * 512],
                           start=(i == 0), stop=False)
        sPE.inc(mm)
        a2mm_pe[i] = sPE.n
        DVE.wait_ge(sPE.h, att_pe[tt])
        DVE.wait_ge(sC2.h, oac_free[bi])
        DVE.tensor_copy(oac[bi][:, :], pco[bi][:, :])
        sDV.inc(DVE.engine_nop())
        pco_free[bi] = sDV.n
        SP.wait_ge(sDV.h, sDV.n)
        sC2.inc(SP.dma_start(oacc_t[tt], oac[bi][:, :]), 16)
        oac_free[bi] = sC2.n

    abs1_ac = sAC.n
    PL.wait_ge(sAC.h, abs1_ac)
    s2h0_ld = load_s2(0)
    a2mm0 = {}
    for i in range(16):
        ACT.wait_ge(sWT.h, s2h0_ld - 16 * (15 - i))
        if i >= 2:
            ACT.wait_ge(sPE.h, a2mm0[i - 2])
        sAC.inc(ACT.activation(ab2[i % 2][:, :], s2t[:, i * 1024:(i + 1) * 1024], AF.Abs))
        PE.wait_ge(sAC.h, sAC.n)
        for nc_ in range(2):
            mm = PE.matmul(pa2[nc_], ones_l[:, 0:1],
                           ab2[i % 2][:, nc_ * 512:(nc_ + 1) * 512],
                           start=False, stop=(i == 15))
        sPE.inc(mm)
        a2mm0[i] = sPE.n
    abs_pe = sPE.n
    for ch in range(4):
        sAC.inc(ACT.activation(s2t[:, ch * 4096:(ch + 1) * 4096],
                               s2t[:, ch * 4096:(ch + 1) * 4096], AF.Sign))
    s2sign = {0: sAC.n}

    DVE.wait_ge(sPE.h, abs_pe)
    DVE.tensor_copy(a2row[0:1, 0:512], pa2[0])
    DVE.tensor_copy(a2row[0:1, 512:1024], pa2[1])
    DVE.engine_nop(); DVE.engine_nop()
    DVE.tensor_scalar(a2row[0:1, :], a2row[0:1, :], 1.0 / 4096.0, None, ALU.mult)
    DVE.engine_nop(); DVE.engine_nop()
    DVE.reciprocal(rcp[0:1, :], a2row[0:1, :])
    DVE.engine_nop(); DVE.engine_nop()
    DVE.tensor_tensor(b2pf[0:1, :], b2rs[0:1, :], rcp[0:1, :], ALU.mult)
    DVE.engine_nop(); DVE.engine_nop()
    DVE.tensor_copy(b2pb[0:1, :], b2pf[0:1, :])
    DVE.tensor_copy(a2hl[0:1, 0:1024], a2row[0:1, :])
    DVE.engine_nop(); DVE.engine_nop()
    DVE.tensor_tensor(a2hl[0:1, 1024:2048], a2row[0:1, :], a2hl[0:1, 0:1024], ALU.subtract)
    sDV.inc(DVE.engine_nop())
    PE.wait_ge(sDV.h, sDV.n)
    PE.wait_ge(sDV.h, pco_free[0])
    for nc_ in range(2):
        PE.matmul(pco[0][:, nc_ * 512:(nc_ + 1) * 512], ones_l[0:1, 0:128],
                  a2hl[0:1, nc_ * 512:(nc_ + 1) * 512], start=True, stop=False)
        mm = PE.matmul(pco[0][:, nc_ * 512:(nc_ + 1) * 512], ones_l[0:1, 0:128],
                       a2hl[0:1, 1024 + nc_ * 512: 1024 + (nc_ + 1) * 512],
                       start=False, stop=True)
    sPE.inc(mm)
    DVE.wait_ge(sPE.h, sPE.n)
    DVE.tensor_copy(a2bc[:, :], pco[0][:, :])
    sDV.inc(DVE.engine_nop())
    pco_free[0] = sDV.n
    a2_done = sDV.n

    m1_end = {}

    def emit_ffn_half(jh):
        gelu_done = sAC.n
        for cc in range(4):
            for jt in range(16):
                bk = jt % 2
                if cc == 0 and jt == 0:
                    PE.wait_ge(sAC.h, s1sign[jh])
                    PE.wait_ge(sDV.h, b_done_dv)
                    ACT.wait_ge(sDV.h, alpha1_done)
                PE.wait_ge(sAC.h, gelu_done if jt < 2 else sAC.n - 1)
                for dt in range(8):
                    mm = PE.matmul(ph[bk],
                                   s1t[:, dt * 2048 + jt * 128: dt * 2048 + jt * 128 + 128],
                                   xnT_own[:, dt * TOWN + cc * 512: dt * TOWN + cc * 512 + 512],
                                   start=(dt == 0), stop=(dt == 7))
                sPE.inc(mm)
                ACT.wait_ge(sPE.h, sPE.n)
                sAC.inc(ACT.activation(hbuf[:, jt * 512:(jt + 1) * 512], ph[bk],
                                       AF.Gelu, bias=b12s[:, jh * 16 + jt: jh * 16 + jt + 1],
                                       scale=alpha1s[:, jh * 16 + jt: jh * 16 + jt + 1]))
            if cc == 3:
                m1_end[jh] = sPE.n
            gelu_done = sAC.n
            for tl in range(4):
                tt = cc * 4 + tl
                bi = tt % 2
                ACT.wait_ge(sDV.h, pco_free[bi])
                sC3.inc(ACT.dma_start(oac[bi][:, :], oacc_t[tt]), 16)
                if jh == 1:
                    sC3.inc(ACT.dma_start(xre[bi][:, :], x_own_t[tt]), 16)
                PE.wait_ge(sAC.h, gelu_done)
                PE.wait_ge(sDV.h, pco_free[bi])
                if cc == 0 and tl == 0:
                    PE.wait_ge(sAC.h, s2sign[jh])
                    if jh == 0:
                        PE.wait_ge(sDV.h, a2_done)
                for nc_ in range(2):
                    if jh == 0:
                        PE.matmul(pco[bi][:, nc_ * 512:(nc_ + 1) * 512], ones_l[0:1, 0:128],
                                  b2pb[0:1, nc_ * 512:(nc_ + 1) * 512], start=True, stop=False)
                    for jt in range(16):
                        mm = PE.matmul(pco[bi][:, nc_ * 512:(nc_ + 1) * 512],
                                       hbuf[:, jt * 512 + tl * 128: jt * 512 + tl * 128 + 128],
                                       s2t[:, jt * 1024 + nc_ * 512: jt * 1024 + nc_ * 512 + 512],
                                       start=(jh == 1 and jt == 0), stop=(jt == 15))
                sPE.inc(mm)
                DVE.wait_ge(sPE.h, sPE.n)
                DVE.wait_ge(sC3.h, sC3.n)
                DVE.wait_ge((sC2 if jh == 0 else sST).h, ost_free[bi])
                DVE.tensor_tensor(ost[bi][:, :], pco[bi][:, :], a2bc[:, :], ALU.mult)
                DVE.engine_nop(); DVE.engine_nop()
                DVE.tensor_tensor(ost[bi][:, :], ost[bi][:, :], oac[bi][:, :], ALU.add)
                if jh == 1:
                    DVE.engine_nop(); DVE.engine_nop()
                    DVE.tensor_tensor(ost[bi][:, :], ost[bi][:, :], xre[bi][:, :], ALU.add)
                sDV.inc(DVE.engine_nop())
                pco_free[bi] = sDV.n
                SP.wait_ge(sDV.h, sDV.n)
                if jh == 0:
                    sC2.inc(SP.dma_start(oacc_t[tt], ost[bi][:, :]), 16)
                    ost_free[bi] = sC2.n
                else:
                    sST.inc(SP.dma_start(y_t[tt], ost[bi][:, :]), 16)
                    ost_free[bi] = sST.n

    ost_free = [sC2.n, sC2.n]
    ACT.wait_ge(sC2.h, sC2.n)
    emit_ffn_half(0)
    PL.wait_ge(sPE.h, m1_end[0])
    s1_ld1 = load_s1(1)
    h0_pe = sPE.n
    PL.wait_ge(sPE.h, h0_pe)
    s2_ld1 = load_s2(1)
    ACT.wait_ge(sWT.h, s1_ld1)
    for ch in range(4):
        sAC.inc(ACT.activation(s1t[:, ch * 4096:(ch + 1) * 4096],
                               s1t[:, ch * 4096:(ch + 1) * 4096], AF.Sign))
    s1sign[1] = sAC.n
    ACT.wait_ge(sWT.h, s2_ld1)
    for ch in range(4):
        sAC.inc(ACT.activation(s2t[:, ch * 4096:(ch + 1) * 4096],
                               s2t[:, ch * 4096:(ch + 1) * 4096], AF.Sign))
    s2sign[1] = sAC.n
    ACT.wait_ge(sC2.h, sC2.n)
    emit_ffn_half(1)

    PL.wait_ge(sST.h, sST.n)
    return nc


def _rope_tables(cols):
    r = np.arange(128)
    i = (r % 64) // 2
    freq = np.exp(-(2.0 * i / 64.0) * np.log(np.float64(10000.0)))
    ang = freq[:, None] * cols[None, :].astype(np.float64)
    c = np.cos(ang)
    s = np.sin(ang) * np.where(r % 2 == 0, -1.0, 1.0)[:, None]
    out = []
    for v in (c, s):
        hi = v.astype(np.float32)
        lo = (v - hi.astype(np.float64)).astype(np.float32)
        out += [hi, lo]
    return out


def _masks(half):
    tk = np.arange(128)[:, None]
    tq = np.arange(512)[None, :]
    tri = np.zeros((128, 2048), dtype=np.float32)
    for j in range(4):
        tri[:, j * 512:(j + 1) * 512] = (tq >= tk + 128 * j).astype(np.float32)
    ones = np.ones((128, 2048), dtype=np.float32)
    zeros = np.zeros((128, 2048), dtype=np.float32)
    m = np.zeros((128, 4 * 2048), dtype=np.float32)
    for cp in range(2):
        a, bb = (tri, zeros) if (cp + half) % 2 == 0 else (ones, tri)
        m[:, (cp * 2) * 2048:(cp * 2 + 1) * 2048] = a
        m[:, (cp * 2 + 1) * 2048:(cp * 2 + 2) * 2048] = bb
    return m


def make_core_inputs(core, I):
    b, half = core // 2, core % 2
    x = np.asarray(I["L0_attn_x"][b], dtype=np.float32)
    rs = OWN_RANGES[half]
    x_own = np.concatenate([x[a:bb] for a, bb in rs], axis=0)
    own_cols = np.concatenate([np.arange(a, bb) for a, bb in rs])
    ch, cl, sh, sl = _rope_tables(np.arange(T))
    cqh_, cql_, sqh_, sql_ = _rope_tables(own_cols)
    f32 = lambda a: np.ascontiguousarray(a, dtype=np.float32)
    return {
        "x_all": f32(x),
        "x_own": f32(x_own),
        "wq": f32(I["L0_attn_Wq"]), "wk": f32(I["L0_attn_Wk"]),
        "wv": f32(I["L0_attn_Wv"]), "wo": f32(I["L0_attn_Wo"]),
        "w1t": f32(np.asarray(I["L0_ffn_w1"]).T),
        "w2t": f32(np.asarray(I["L0_ffn_w2"]).T),
        "w1n": f32(I["L0_ffn_w1"]),
        "bq2": f32(np.asarray(I["L0_attn_bq"]).reshape(8, 128).T),
        "bk2": f32(np.asarray(I["L0_attn_bk"]).reshape(2, 128).T),
        "b12": f32(np.asarray(I["L0_ffn_b1"]).reshape(32, 128).T),
        "bvr": f32(np.asarray(I["L0_attn_bv"]).reshape(1, 256)),
        "bor": f32(np.asarray(I["L0_attn_bo"]).reshape(1, D)),
        "b2r": f32(np.asarray(I["L0_ffn_b2"]).reshape(1, D)),
        "ckh": f32(ch), "ckl": f32(cl), "skh": f32(sh), "skl": f32(sl),
        "cqh": f32(cqh_), "cql": f32(cql_), "sqh": f32(sqh_), "sql": f32(sql_),
        "mask4": _masks(half),
    }


_NC_CACHE = {}


def kernel(**inputs):
    key = (_STAGE, _DEBUG)
    if key not in _NC_CACHE:
        _NC_CACHE[key] = build_program(stage=_STAGE, debug=_DEBUG)
    nc = _NC_CACHE[key]
    in_maps = [make_core_inputs(c, inputs) for c in range(8)]
    res = run_bass_kernel_spmd(nc, in_maps, core_ids=list(range(8)))
    out = np.zeros((4, T, D), dtype=np.float32)
    for c in range(8):
        b, half = c // 2, c % 2
        yv = np.asarray(res.results[c]["y"])
        o = 0
        for a, bb in OWN_RANGES[half]:
            out[b, a:bb] = yv[o:o + (bb - a)]
            o += bb - a
    return out
